# version 02
# baseline (speedup 1.0000x reference)
import contextlib
import numpy as np
import ml_dtypes
import concourse.bass as bass
import concourse.mybir as mybir
from concourse.bass_utils import run_bass_kernel_spmd

F32 = mybir.dt.float32
BF16 = mybir.dt.bfloat16
AF = mybir.ActivationFunctionType
ALU = mybir.AluOpType

B, S, D = 2, 8192, 1024
W = 256
CH = 2048
TP = CH + W
TC = 384
NTC = TP // TC
NPT = D // 128
NTT = TP // 128
BCH = 512
NBC = S // BCH
NHP = 8
NKT = S // 128
HD = 64
I_FF = 2048
NPI = I_FF // 128
NE = 8
EPS_OUT = 1e-5
EPS_IN = 1e-6
BANK = 512


class Prog:
    def __init__(self, nc, block, es):
        self.nc, self.block, self.es = nc, block, es
        self._sems = {}

    def sem(self, name):
        if name not in self._sems:
            self._sems[name] = self.es.enter_context(self.nc.semaphore(name))
        return self._sems[name]

    def vg(self, v, inst):
        self._vch = getattr(self, "_vch", 0) + 1
        inst.then_inc(self.sem("vch"), 1)
        v.wait_ge(self.sem("vch"), self._vch)


class Chain:

    def __init__(self, p, tag):
        self.p = p
        self.tag = tag
        self.cums = {}
        self.steps = []
        self.last = None

    def _bump(self, name, amt):
        v = self.cums.get(name, 0) + amt
        self.cums[name] = v
        return v

    def step(self, eng, fn, dma=False):
        semname = (self.tag + "_D" + eng) if dma else self.tag
        amt = 16 if dma else 1
        v = self._bump(semname, amt)
        self.steps.append((eng, fn, semname, amt, self.last, dma))
        self.last = (semname, v)

    def tail(self):
        return self.last

    def flush(self):
        per_eng = {}
        for (eng, fn, semname, amt, wait, dma) in self.steps:
            per_eng.setdefault(eng, []).append((wait, fn, semname, amt))
        for eng_name, items in per_eng.items():
            def section(e, items=items):
                for wait, fn, semname, amt in items:
                    if wait is not None:
                        e.wait_ge(self.p.sem(wait[0]), wait[1])
                    fn(e).then_inc(self.p.sem(semname), amt)
            getattr(self.p.block, eng_name)(section)
        self.steps = []


def _last(*insts):
    return insts[-1]


def build_program(stages=("A",), dbg_out=None):
    nc = bass.Bass("TRN2", target_bir_lowering=False)
    es = contextlib.ExitStack()

    dram = {}
    def din(name, shape, dt=F32):
        dram[name] = nc.dram_tensor(name, shape, dt, kind="ExternalInput").ap()
        return dram[name]
    def dint(name, shape, dt):
        dram[name] = nc.dram_tensor(name, shape, dt, kind="Internal").ap()
        return dram[name]

    din("ident", [128, 128])
    dint("xres", [128, NPT * TP], F32)
    if "IN" in stages:
        din("xs", [TP, D])
    else:
        din("xres_in", [128, NPT * TP], F32)
    if "C" in stages or "D" in stages:
        din("mask", [128, TP])
    if "C" in stages:
        din("l1_w", [D]); din("l1_u", [D]); din("l1_mk", [D]); din("l1_mv", [D]); din("l1_mr", [D])
        din("Wk_T", [D, D]); din("Wv_T", [D, D]); din("Wr_T", [D, D]); din("Wo_T", [D, D])
        dint("kf_scr", [D, TP], F32)
        dint("vf_scr", [D, TP], F32)
        dint("sr_scr", [D, TP], BF16)
        dint("rw_scr", [D, TP], BF16)
    if "D" in stages:
        din("f_Wr_T", [D, D]); din("f_Wk_T", [D, D]); din("f_Wv_T", [D, D])
        din("f_mur", [D]); din("f_muk", [D])
    if "B" in stages:
        din("sh_gate_T", [D, I_FF]); din("sh_up_T", [D, I_FF]); din("sh_down_T", [I_FF, D])
        din("gate_w", [NE, D, I_FF]); din("up_w", [NE, D, I_FF]); din("down_w", [NE, I_FF, D])
        din("router_T", [D, NE]); din("r_bias", [NE])
        dint("ln1_scr", [D, TP], BF16)
    if "A" in stages:
        din("xb", [S, D])
        din("w_q", [D, D]); din("w_k", [D, D]); din("w_v", [D, D]); din("w_proj", [D, D])
        din("b_proj", [D]); din("ln_g", [D]); din("ln_b", [D])
        dint("kT_scr", [D, S], BF16)
        dint("v_scr", [NHP, S, 130], BF16)
        dint("qT_scr", [D, TP], BF16)
        dint("oT_scr", [D, TP], BF16)

    out_names = []
    if "OUT" in stages:
        dram["y"] = nc.dram_tensor("y", [CH, D], F32, kind="ExternalOutput").ap()
        out_names.append("y")
    if dbg_out:
        dram[dbg_out] = nc.dram_tensor(dbg_out, [128, NPT * TP], F32,
                                       kind="ExternalOutput").ap()
        out_names.append(dbg_out)

    with es:
        blk = es.enter_context(nc.Block())
        p = Prog(nc, blk, es)
        sbuf = lambda nm, shp, dt: es.enter_context(nc.sbuf_tensor(nm, shp, dt))

        h = {"dram": dram}
        h["ones1f"] = ones1f = sbuf("ones1f", [128, 1], F32)
        h["onesrf"] = onesrf = sbuf("onesrf", [1, 128], F32)
        h["idf"] = idf = sbuf("idf", [128, 128], F32)
        h["idb"] = idb = sbuf("idb", [128, 128], BF16)
        h["epsA"] = epsA = sbuf("epsA", [1, 1], F32)
        h["epsB"] = epsB = sbuf("epsB", [1, 1], F32)
        rows = sbuf("rows", [1, TP + 2 * BCH], F32)
        h["RTMP"] = rows[0:1, 0:TP]
        h["RB0"] = rows[0:1, TP:TP + BCH]
        h["RB1"] = rows[0:1, TP + BCH:TP + 2 * BCH]
        h["m0"] = sbuf("m0", [1, TP], F32)
        h["r0"] = sbuf("r0", [1, TP], F32)
        PSf = es.enter_context(nc.psum_tensor("PSf", [128, 8 * BANK], F32))
        h["PSb"] = PSb = lambda k, off=0, n=BANK: PSf[:, k * BANK + off: k * BANK + off + n]
        h["PSf"] = PSf
        h["sbuf"] = sbuf
        h["nc"], h["p"], h["blk"] = nc, p, blk

        C = Chain(p, "main")
        h["C"] = C

        self_p = p
        def ln_stats_rows(ch, src_tiles, npt, n, sq_buf, mrow, rrow, trow, eps_ap, nelem):
            for pt in range(npt):
                ch.step("tensor", lambda t, pt=pt: t.matmul(
                    PSb(4, 0, n)[0:1, :], ones1f[:], src_tiles(pt),
                    start=(pt == 0), stop=(pt == npt - 1)))
                ch.step("scalar", lambda s, pt=pt: s.activation(
                    sq_buf[:, 0:n], src_tiles(pt), AF.Square))
                ch.step("tensor", lambda t, pt=pt: t.matmul(
                    PSb(5, 0, n)[0:1, :], ones1f[:], sq_buf[:, 0:n],
                    start=(pt == 0), stop=(pt == npt - 1)))
            def v_rows(v):
                i1 = v.tensor_scalar_mul(mrow, PSb(4, 0, n)[0:1, :], 1.0 / nelem)
                v.tensor_scalar_mul(trow, PSb(5, 0, n)[0:1, :], 1.0 / nelem)
                self_p.vg(v, i1)
                i2 = v.tensor_tensor(rrow, mrow, mrow, ALU.mult)
                self_p.vg(v, i2)
                return v.tensor_tensor(trow, trow, rrow, ALU.subtract)
            ch.step("vector", v_rows)
            ch.step("scalar", lambda s: s.activation(trow, trow, AF.Sqrt, bias=eps_ap[:, 0:1]))
            ch.step("vector", lambda v: v.reciprocal(rrow, trow))
        h["ln_stats_rows"] = ln_stats_rows

        def bcast_row(ch, row_ap, n, bank):
            ch.step("tensor", lambda t: t.matmul(PSb(bank, 0, n), onesrf[:], row_ap,
                                                 start=True, stop=True))
        h["bcast_row"] = bcast_row

        def normalize(ch, src_tiles, dst_tiles, npt, n, mb, rb):
            def v_norm(v):
                for pt in range(npt):
                    d = dst_tiles(pt)
                    i0 = v.tensor_tensor(d, src_tiles(pt), mb, ALU.subtract)
                    self_p.vg(v, i0)
                    i1 = v.tensor_tensor(d, d, rb, ALU.mult)
                return i1
            ch.step("vector", v_norm)
        h["normalize"] = normalize

        def mm_acc(t, psum_ap, lhsT_fn, rhs_fn, nk):
            for kt in range(nk):
                i1 = t.matmul(psum_ap, lhsT_fn(kt), rhs_fn(kt),
                              start=(kt == 0), stop=(kt == nk - 1))
            return i1
        h["mm_acc"] = mm_acc

        C.step("vector", lambda v: _last(
            v.memset(ones1f[:], 1.0), v.memset(onesrf[:], 1.0),
            v.memset(epsA[:], EPS_OUT), v.memset(epsB[:], EPS_IN)))
        C.step("sync", lambda sy: sy.dma_start(out=idf[:], in_=dram["ident"]), dma=True)
        C.step("gpsimd", lambda g: g.dma_start(out=idb[:], in_=dram["ident"]), dma=True)
        C.flush()

        if "IN" not in stages:
            C.step("sync", lambda sy: sy.dma_start(out=dram["xres"],
                                                   in_=dram["xres_in"]), dma=True)
        if "A" in stages:
            emit_stage_a(h)
        if "B" in stages:
            emit_stage_b(h)
        if "C" in stages:
            emit_stage_c(h)
        if "D" in stages:
            emit_stage_d(h)
        if "OUT" in stages:
            emit_output(h)

        if dbg_out:
            C.flush()
            tl = C.tail()
            st = p.sem("st_dbg")

            @blk.sync
            def _(sy):
                if tl:
                    sy.wait_ge(p.sem(tl[0]), tl[1])
                sy.dma_start(out=dram[dbg_out], in_=dram["xres"]).then_inc(st, 16)
                sy.wait_ge(st, 16)

        C.flush()
        finals = dict(C.cums)

        @blk.sync
        def _(sy):
            for name, val in finals.items():
                sy.wait_ge(p.sem(name), val)
    return nc, out_names


def emit_transpose_in(h, xT):
    nc, p, blk, C = h["nc"], h["p"], h["blk"], h["C"]
    PSb, idf = h["PSb"], h["idf"]
    d_src = h["dram"]["xs"]
    with contextlib.ExitStack() as ph:
        xstage = ph.enter_context(nc.sbuf_tensor("xstage", [128, 2 * D], F32))
        xld = [p.sem("xld0"), p.sem("xld1")]
        xtp, xcp = p.sem("xtp"), p.sem("xcp")
        C.flush()
        tl = C.tail()

        @blk.sync
        def _(sy):
            if tl:
                sy.wait_ge(p.sem(tl[0]), tl[1])
            for tt in range(NTT):
                if tt >= 2:
                    sy.wait_ge(xcp, tt - 1)
                sy.dma_start(out=xstage[:, (tt % 2) * D:(tt % 2 + 1) * D],
                             in_=d_src[tt * 128:(tt + 1) * 128, :]).then_inc(xld[tt % 2], 16)

        @blk.tensor
        def _(t):
            if tl:
                t.wait_ge(p.sem(tl[0]), tl[1])
            for tt in range(NTT):
                t.wait_ge(xld[tt % 2], (tt // 2 + 1) * 16)
                if tt >= 2:
                    t.wait_ge(xcp, tt - 1)
                for pt in range(NPT):
                    bank = (tt % 2) * 2 + pt // 4
                    ins = t.transpose(PSb(bank, (pt % 4) * 128, 128),
                                      xstage[:, (tt % 2) * D + pt * 128:
                                             (tt % 2) * D + (pt + 1) * 128],
                                      idf[:])
                ins.then_inc(xtp, 1)

        @blk.vector
        def _(v):
            for tt in range(NTT):
                v.wait_ge(xtp, tt + 1)
                for pt in range(NPT):
                    bank = (tt % 2) * 2 + pt // 4
                    ins = v.tensor_copy(xT[:, pt * TP + tt * 128: pt * TP + (tt + 1) * 128],
                                        PSb(bank, (pt % 4) * 128, 128))
                ins.then_inc(xcp, 1)

        C.step("vector", lambda v: _last(v.wait_ge(xcp, NTT),
                                         v.tensor_copy(h["idf"][0:1, 0:1], h["idf"][0:1, 0:1])))
        C.flush()


def emit_stage_a(h):
    nc, p, blk, C, dram = h["nc"], h["p"], h["blk"], h["C"], h["dram"]
    PSb, PSf, sbuf = h["PSb"], h["PSf"], h["sbuf"]
    ones1f, onesrf, idf, idb, epsA, epsB = (h[k] for k in
                                            ("ones1f", "onesrf", "idf", "idb", "epsA", "epsB"))
    RTMP, RB0, RB1, m0, r0 = h["RTMP"], h["RB0"], h["RB1"], h["m0"], h["r0"]
    ln_stats_rows, bcast_row, normalize, mm_acc = (
        h[k] for k in ("ln_stats_rows", "bcast_row", "normalize", "mm_acc"))
    d_kT, d_vh, d_qT, d_oT, d_xres = (dram[k] for k in
                                      ("kT_scr", "v_scr", "qT_scr", "oT_scr", "xres"))
    Wt = lambda w, kt, mt: w[:, kt * D + mt * 128: kt * D + (mt + 1) * 128]

    vecs = sbuf("vecs_a", [128, 24], F32)
    for i, nm in enumerate(("b_proj", "ln_g", "ln_b")):
        C.step("sync", lambda sy, i=i, nm=nm: sy.dma_start(
            out=vecs[:, i * 8:(i + 1) * 8],
            in_=dram[nm].rearrange("(t p) -> p t", p=128),
            allow_slow_non_contiguous=True), dma=True)
    BPJ, LNG, LNB = vecs[:, 0:8], vecs[:, 8:16], vecs[:, 16:24]

    with contextlib.ExitStack() as s1:
        xT = s1.enter_context(nc.sbuf_tensor("xT", [128, NPT * TP], F32))
        XT = lambda pt, tc: xT[:, pt * TP + tc * TC: pt * TP + (tc + 1) * TC]
        emit_transpose_in(h, xT)

        with contextlib.ExitStack() as ph:
            sq = ph.enter_context(nc.sbuf_tensor("sq_a", [128, TC], F32))
            for tc in range(NTC):
                ln_stats_rows(C, lambda pt, tc=tc: XT(pt, tc), NPT, TC, sq,
                              m0[:, tc * TC:(tc + 1) * TC], r0[:, tc * TC:(tc + 1) * TC],
                              RTMP[:, 0:TC], epsA, D)
            C.flush()

        with contextlib.ExitStack() as ph:
            wk_sb = ph.enter_context(nc.sbuf_tensor("wk_sb", [128, NPT * D], BF16))
            wv_sb = ph.enter_context(nc.sbuf_tensor("wv_sb", [128, NPT * D], BF16))
            bstage = ph.enter_context(nc.sbuf_tensor("bstage", [128, 4 * D], F32))
            bT = ph.enter_context(nc.sbuf_tensor("bT", [128, NPT * BCH], F32))
            lnb = ph.enter_context(nc.sbuf_tensor("lnb", [128, NPT * BCH], BF16))
            kst = ph.enter_context(nc.sbuf_tensor("kst", [128, NPT * BCH], BF16))
            vst = ph.enter_context(nc.sbuf_tensor("vst", [128, NPT * BCH], F32))
            vks = ph.enter_context(nc.sbuf_tensor("vks", [128, 4 * 130], BF16))
            sqb = bstage

            for dst, nm in ((wk_sb, "w_k"), (wv_sb, "w_v")):
                C.step("gpsimd", lambda g, dst=dst, nm=nm: g.dma_start(
                    out=dst[:].rearrange("p (k n) -> p k n", n=D),
                    in_=dram[nm].rearrange("(k p) n -> p k n", p=128)), dma=True)
            C.step("vector", lambda v: v.memset(vks[:], 1.0))

            NT4 = BCH // 128
            ntiles = NT4 * NPT
            for bc in range(NBC):
                C.step("sync", lambda sy, bc=bc: sy.dma_start(
                    out=bstage[:, 0:NT4 * D].rearrange("p (t n) -> p t n", n=D),
                    in_=dram["xb"][bc * BCH:(bc + 1) * BCH, :].rearrange(
                        "(t p) n -> p t n", p=128)), dma=True)
                for g0 in range(0, ntiles, 16):
                    gn = min(16, ntiles - g0)
                    def t_tph(t, g0=g0, gn=gn):
                        for i in range(gn):
                            pt, tt4 = divmod(g0 + i, NT4)
                            ins = t.transpose(PSb(i // 4, (i % 4) * 128, 128),
                                              bstage[:, tt4 * D + pt * 128:
                                                     tt4 * D + (pt + 1) * 128],
                                              idf[:])
                        return ins
                    C.step("tensor", t_tph)
                    def v_cph(v, g0=g0, gn=gn):
                        for i in range(gn):
                            pt, tt4 = divmod(g0 + i, NT4)
                            ins = v.tensor_copy(
                                bT[:, pt * BCH + tt4 * 128: pt * BCH + (tt4 + 1) * 128],
                                PSb(i // 4, (i % 4) * 128, 128))
                        return ins
                    C.step("vector", v_cph)
                ln_stats_rows(C, lambda pt: bT[:, pt * BCH:(pt + 1) * BCH], NPT, BCH,
                              sqb, RB0, RB1, RTMP[:, 0:BCH], epsA, D)
                bcast_row(C, RB0, BCH, 6)
                bcast_row(C, RB1, BCH, 7)
                normalize(C, lambda pt: bT[:, pt * BCH:(pt + 1) * BCH],
                          lambda pt: lnb[:, pt * BCH:(pt + 1) * BCH], NPT, BCH,
                          PSb(6, 0, BCH), PSb(7, 0, BCH))
                for mt in range(NPT):
                    C.step("tensor", lambda t, mt=mt: mm_acc(
                        t, PSb(mt % 2, 0, BCH), lambda kt: Wt(wk_sb, kt, mt),
                        lambda kt: lnb[:, kt * BCH:(kt + 1) * BCH], NPT))
                    C.step("vector", lambda v, mt=mt: v.tensor_copy(
                        kst[:, mt * BCH:(mt + 1) * BCH], PSb(mt % 2, 0, BCH)))
                    C.step("tensor", lambda t, mt=mt: mm_acc(
                        t, PSb(2 + mt % 2, 0, BCH), lambda kt: Wt(wv_sb, kt, mt),
                        lambda kt: lnb[:, kt * BCH:(kt + 1) * BCH], NPT))
                    C.step("vector", lambda v, mt=mt: v.tensor_copy(
                        vst[:, mt * BCH:(mt + 1) * BCH], PSb(2 + mt % 2, 0, BCH)))
                for mt in range(NPT):
                    C.step("sync", lambda sy, bc=bc, mt=mt: sy.dma_start(
                        out=d_kT[mt * 128:(mt + 1) * 128, bc * BCH:(bc + 1) * BCH],
                        in_=kst[:, mt * BCH:(mt + 1) * BCH]), dma=True)
                for pt in range(NPT):
                    C.step("tensor", lambda t, pt=pt: _last(*[
                        t.transpose(PSb(4 + tt4 // 4, (tt4 % 4) * 128, 128),
                                    vst[:, pt * BCH + tt4 * 128: pt * BCH + (tt4 + 1) * 128],
                                    idf[:]) for tt4 in range(NT4)]))
                    def v_vcp(v, pt=pt):
                        for tt4 in range(NT4):
                            v.tensor_copy(vks[:, tt4 * 130 + 0: tt4 * 130 + 64],
                                          PSb(4 + tt4 // 4, (tt4 % 4) * 128, 64))
                            ins = v.tensor_copy(vks[:, tt4 * 130 + 65: tt4 * 130 + 129],
                                                PSb(4 + tt4 // 4, (tt4 % 4) * 128 + 64, 64))
                        return ins
                    C.step("vector", v_vcp)
                    C.step("sync", lambda sy, bc=bc, pt=pt: sy.dma_start(
                        out=d_vh[pt, bc * BCH:(bc + 1) * BCH, :].rearrange(
                            "(t p) c -> p t c", p=128),
                        in_=vks[:, 0:NT4 * 130].rearrange("p (t c) -> p t c", c=130)), dma=True)
                C.flush()

        with contextlib.ExitStack() as ph:
            wq_sb = ph.enter_context(nc.sbuf_tensor("wq_sb", [128, NPT * D], BF16))
            lnq = ph.enter_context(nc.sbuf_tensor("lnq", [128, NPT * TC], BF16))
            qst = ph.enter_context(nc.sbuf_tensor("qst", [128, TC], BF16))
            C.step("gpsimd", lambda g: g.dma_start(
                out=wq_sb[:].rearrange("p (k n) -> p k n", n=D),
                in_=dram["w_q"].rearrange("(k p) n -> p k n", p=128)), dma=True)
            for tc in range(NTC):
                bcast_row(C, m0[:, tc * TC:(tc + 1) * TC], TC, 6)
                bcast_row(C, r0[:, tc * TC:(tc + 1) * TC], TC, 7)
                normalize(C, lambda pt, tc=tc: XT(pt, tc),
                          lambda pt: lnq[:, pt * TC:(pt + 1) * TC], NPT, TC,
                          PSb(6, 0, TC), PSb(7, 0, TC))
                for mt in range(NPT):
                    C.step("tensor", lambda t, mt=mt: mm_acc(
                        t, PSb(mt % 2, 0, TC), lambda kt: Wt(wq_sb, kt, mt),
                        lambda kt: lnq[:, kt * TC:(kt + 1) * TC], NPT))
                    C.step("vector", lambda v, mt=mt: v.tensor_copy(
                        qst[:], PSb(mt % 2, 0, TC)))
                    C.step("sync", lambda sy, mt=mt, tc=tc: sy.dma_start(
                        out=d_qT[mt * 128:(mt + 1) * 128, tc * TC:(tc + 1) * TC],
                        in_=qst[:]), dma=True)
                C.flush()

        C.step("sync", lambda sy: sy.dma_start(out=d_xres, in_=xT[:]), dma=True)
        C.flush()

    emit_attention(h, d_kT, d_vh, d_qT, d_oT)

    with contextlib.ExitStack() as s3:
        xT = s3.enter_context(nc.sbuf_tensor("xT5", [128, NPT * TP], F32))
        ybuf = s3.enter_context(nc.sbuf_tensor("ybuf", [128, NPT * TP], F32))
        wp_sb = s3.enter_context(nc.sbuf_tensor("wp_sb", [128, NPT * D], BF16))
        och = s3.enter_context(nc.sbuf_tensor("och", [128, NPT * TC], BF16))
        sq5 = s3.enter_context(nc.sbuf_tensor("sq5", [128, TC], F32))
        XT = lambda pt, tc: xT[:, pt * TP + tc * TC: pt * TP + (tc + 1) * TC]
        YT = lambda pt, tc: ybuf[:, pt * TP + tc * TC: pt * TP + (tc + 1) * TC]
        m1, r1 = m0, r0

        C.step("sync", lambda sy: sy.dma_start(out=xT[:], in_=d_xres), dma=True)
        C.step("gpsimd", lambda g: g.dma_start(
            out=wp_sb[:].rearrange("p (k n) -> p k n", n=D),
            in_=dram["w_proj"].rearrange("(k p) n -> p k n", p=128)), dma=True)

        for tc in range(NTC):
            C.step("sync", lambda sy, tc=tc: sy.dma_start(
                out=och[:].rearrange("p (k n) -> p k n", n=TC),
                in_=d_oT[:, tc * TC:(tc + 1) * TC].rearrange("(k p) n -> p k n", p=128)), dma=True)
            bcast_row(C, m0[:, tc * TC:(tc + 1) * TC], TC, 6)
            bcast_row(C, r0[:, tc * TC:(tc + 1) * TC], TC, 7)
            for mt in range(NPT):
                C.step("tensor", lambda t, mt=mt, tc=tc: mm_acc(
                    t, PSb(mt % 2, 0, TC), lambda kt: Wt(wp_sb, kt, mt),
                    lambda kt: och[:, kt * TC:(kt + 1) * TC], NPT))
                def v_y(v, mt=mt, tc=tc):
                    y = YT(mt, tc)
                    i0 = v.tensor_scalar_add(y, PSb(mt % 2, 0, TC), BPJ[:, mt:mt + 1])
                    p.vg(v, i0)
                    i1 = v.tensor_tensor(sq5[:], XT(mt, tc), PSb(6, 0, TC), ALU.subtract)
                    p.vg(v, i1)
                    i2 = v.tensor_tensor(sq5[:], sq5[:], PSb(7, 0, TC), ALU.mult)
                    p.vg(v, i2)
                    return v.tensor_tensor(y, y, sq5[:], ALU.add)
                C.step("vector", v_y)
            C.flush()
        for tc in range(NTC):
            ln_stats_rows(C, lambda pt, tc=tc: YT(pt, tc), NPT, TC, sq5,
                          m1[:, tc * TC:(tc + 1) * TC], r1[:, tc * TC:(tc + 1) * TC],
                          RTMP[:, 0:TC], epsB, D)
        C.flush()
        for tc in range(NTC):
            bcast_row(C, m1[:, tc * TC:(tc + 1) * TC], TC, 6)
            bcast_row(C, r1[:, tc * TC:(tc + 1) * TC], TC, 7)
            def v_fin(v, tc=tc):
                for pt in range(NPT):
                    n = sq5[:]
                    i0 = v.tensor_tensor(n, YT(pt, tc), PSb(6, 0, TC), ALU.subtract)
                    p.vg(v, i0)
                    i1 = v.tensor_tensor(n, n, PSb(7, 0, TC), ALU.mult)
                    p.vg(v, i1)
                    i2 = v.scalar_tensor_tensor(XT(pt, tc), n, LNG[:, pt:pt + 1], XT(pt, tc),
                                                ALU.mult, ALU.add)
                    p.vg(v, i2)
                    i3 = v.tensor_scalar_add(XT(pt, tc), XT(pt, tc), LNB[:, pt:pt + 1])
                return i3
            C.step("vector", v_fin)
            C.flush()
        C.step("sync", lambda sy: sy.dma_start(out=d_xres, in_=xT[:]), dma=True)
        C.flush()


def emit_stage_b(h):
    nc, p, blk, C, dram = h["nc"], h["p"], h["blk"], h["C"], h["dram"]
    PSb, PSf, sbuf = h["PSb"], h["PSf"], h["sbuf"]
    ones1f, onesrf, idf, epsA = h["ones1f"], h["onesrf"], h["idf"], h["epsA"]
    RTMP, m0, r0 = h["RTMP"], h["m0"], h["r0"]
    ln_stats_rows, bcast_row, normalize = (h[k] for k in
                                           ("ln_stats_rows", "bcast_row", "normalize"))
    d_xres, d_ln1 = dram["xres"], dram["ln1_scr"]
    NKI = I_FF // 128
    w8T = sbuf("w8T", [8, TP], F32)

    with contextlib.ExitStack() as s1:
        xT = s1.enter_context(nc.sbuf_tensor("xTb", [128, NPT * TP], F32))
        XT = lambda pt, tc: xT[:, pt * TP + tc * TC: pt * TP + (tc + 1) * TC]
        sq = s1.enter_context(nc.sbuf_tensor("sq_b", [128, TC], F32))
        lnc = s1.enter_context(nc.sbuf_tensor("lnc_b", [128, NPT * TC], BF16))
        rtf = s1.enter_context(nc.sbuf_tensor("rtf_b", [128, NPT * NE], F32))
        lnf = s1.enter_context(nc.sbuf_tensor("lnf_b", [128, NPT * TC], F32))
        pr = s1.enter_context(nc.sbuf_tensor("pr_b", [128, NTT * NE], F32))
        prb = s1.enter_context(nc.sbuf_tensor("prb_b", [128, NTT * NE], F32))
        w8 = s1.enter_context(nc.sbuf_tensor("w8_b", [128, NTT * NE], F32))
        neg = s1.enter_context(nc.sbuf_tensor("neg_b", [128, NE], F32))
        rb1 = s1.enter_context(nc.sbuf_tensor("rb1_b", [1, NE], F32))
        rbb = s1.enter_context(nc.sbuf_tensor("rbb_b", [128, NE], F32))
        rowm = s1.enter_context(nc.sbuf_tensor("rowm_b", [128, 4], F32))
        mskd = s1.enter_context(nc.sbuf_tensor("mskd_b", [128, NE], F32))
        gei = s1.enter_context(nc.sbuf_tensor("gei_b", [128, NE], mybir.dt.int32))

        C.step("sync", lambda sy: sy.dma_start(out=xT[:], in_=d_xres), dma=True)
        for tc in range(NTC):
            ln_stats_rows(C, lambda pt, tc=tc: XT(pt, tc), NPT, TC, sq,
                          m0[:, tc * TC:(tc + 1) * TC], r0[:, tc * TC:(tc + 1) * TC],
                          RTMP[:, 0:TC], epsA, D)
        C.flush()

        C.step("sync", lambda sy: sy.dma_start(out=rb1[:], in_=dram["r_bias"][None, :]),
               dma=True)
        C.step("sync", lambda sy: sy.dma_start(
            out=rtf[:].rearrange("p (k n) -> p k n", n=NE),
            in_=dram["router_T"].rearrange("(k p) n -> p k n", p=128)), dma=True)
        bcast_row(C, rb1[0:1, :], NE, 5)
        C.step("vector", lambda v: _last(v.memset(neg[:], -1e30),
                                         v.tensor_copy(rbb[:], PSb(5, 0, NE))))
        for tc in range(NTC):
            bcast_row(C, m0[:, tc * TC:(tc + 1) * TC], TC, 6)
            bcast_row(C, r0[:, tc * TC:(tc + 1) * TC], TC, 7)
            normalize(C, lambda pt, tc=tc: XT(pt, tc),
                      lambda pt: lnf[:, pt * TC:(pt + 1) * TC], NPT, TC,
                      PSb(6, 0, TC), PSb(7, 0, TC))
            for st3 in range(TC // 128):
                tt = tc * (TC // 128) + st3
                C.step("tensor", lambda t, st3=st3: _last(*[
                    t.matmul(PSb(4, 0, NE)[0:128, :],
                             lnf[:, kt * TC + st3 * 128: kt * TC + (st3 + 1) * 128],
                             rtf[:, kt * NE:(kt + 1) * NE],
                             start=(kt == 0), stop=(kt == NPT - 1)) for kt in range(NPT)]))
                def v_lg(v, tt=tt):
                    return v.tensor_tensor(pr[:, tt * NE:(tt + 1) * NE],
                                           PSb(4, 0, NE)[0:128, :], rbb[:], ALU.add)
                C.step("vector", v_lg)
                C.step("scalar", lambda s, tt=tt: s.activation(
                    prb[:, tt * NE:(tt + 1) * NE], pr[:, tt * NE:(tt + 1) * NE],
                    AF.Sigmoid))
                def v_top2(v, tt=tt):
                    lz = pr[:, tt * NE:(tt + 1) * NE]
                    pz = prb[:, tt * NE:(tt + 1) * NE]
                    wz = w8[:, tt * NE:(tt + 1) * NE]
                    i1 = v.reduce_max(rowm[:, 0:1], lz, axis=mybir.AxisListType.X)
                    p.vg(v, i1)
                    i2 = v.tensor_scalar(gei[:], lz, rowm[:, 0:1], None, ALU.is_ge)
                    p.vg(v, i2)
                    i3 = v.select(mskd[:], gei[:], neg[:], lz, add_drain=True)
                    p.vg(v, i3)
                    return v.reduce_max(rowm[:, 1:2], mskd[:], axis=mybir.AxisListType.X)
                C.step("vector", v_top2)
                C.step("scalar", lambda s: s.activation(rowm[:, 2:4], rowm[:, 0:2],
                                                        AF.Sigmoid))
                def v_top2b(v, tt=tt):
                    lz = pr[:, tt * NE:(tt + 1) * NE]
                    pz = prb[:, tt * NE:(tt + 1) * NE]
                    wz = w8[:, tt * NE:(tt + 1) * NE]
                    i5 = v.tensor_tensor(rowm[:, 2:3], rowm[:, 2:3], rowm[:, 3:4], ALU.add)
                    p.vg(v, i5)
                    i6 = v.reciprocal(rowm[:, 3:4], rowm[:, 2:3])
                    p.vg(v, i6)
                    i7 = v.tensor_scalar(wz, lz, rowm[:, 1:2], None, ALU.is_ge)
                    p.vg(v, i7)
                    i8 = v.tensor_tensor(wz, wz, pz, ALU.mult)
                    p.vg(v, i8)
                    return v.tensor_scalar(wz, wz, rowm[:, 3:4], None, ALU.mult)
                C.step("vector", v_top2b)
                C.step("tensor", lambda t, tt=tt: t.transpose(
                    PSb(6, 0, 128)[0:NE, :], w8[:, tt * NE:(tt + 1) * NE], idf[:]))
                C.step("vector", lambda v, tt=tt: v.tensor_copy(
                    w8T[:, tt * 128:(tt + 1) * 128], PSb(6, 0, 128)[0:NE, :]))
            def v_cast(v, tc=tc):
                for pt in range(NPT):
                    i1 = v.tensor_copy(lnc[:, pt * TC:(pt + 1) * TC],
                                       lnf[:, pt * TC:(pt + 1) * TC])
                return i1
            C.step("vector", v_cast)
            for pt in range(NPT):
                C.step("sync", lambda sy, pt=pt, tc=tc: sy.dma_start(
                    out=d_ln1[pt * 128:(pt + 1) * 128, tc * TC:(tc + 1) * TC],
                    in_=lnc[:, pt * TC:(pt + 1) * TC]), dma=True)
            C.flush()

    with contextlib.ExitStack() as ph:
        wg = ph.enter_context(nc.sbuf_tensor("wg_b", [128, NPT * I_FF], BF16))
        wu = ph.enter_context(nc.sbuf_tensor("wu_b", [128, NPT * I_FF], BF16))
        wd = ph.enter_context(nc.sbuf_tensor("wd_b", [128, NKI * D], BF16))
        lntc = ph.enter_context(nc.sbuf_tensor("lntc_b", [128, NPT * TC], BF16))
        gu = ph.enter_context(nc.sbuf_tensor("gu_b", [128, NKI * TC], BF16))
        wbb = ph.enter_context(nc.sbuf_tensor("wbb_b", [128, TC], F32))
        wrow = ph.enter_context(nc.sbuf_tensor("wrow_b", [1, TC], F32))
        tmp5 = ph.enter_context(nc.sbuf_tensor("tmp5_b", [128, TC], F32))
        WgT = lambda kt, mt: wg[:, kt * I_FF + mt * 128: kt * I_FF + (mt + 1) * 128]
        WuT = lambda kt, mt: wu[:, kt * I_FF + mt * 128: kt * I_FF + (mt + 1) * 128]
        WdT = lambda kt, mt: wd[:, kt * D + mt * 128: kt * D + (mt + 1) * 128]

        for e in range(-1, NE):
            if e < 0:
                srcs = (dram["sh_gate_T"], dram["sh_up_T"], dram["sh_down_T"])
            else:
                srcs = (dram["gate_w"][e], dram["up_w"][e], dram["down_w"][e])
            for dst, sap, inner in ((wg, srcs[0], I_FF), (wu, srcs[1], I_FF),
                                    (wd, srcs[2], D)):
                C.step("gpsimd", lambda g, dst=dst, sap=sap, inner=inner: g.dma_start(
                    out=dst[:].rearrange("p (k n) -> p k n", n=inner),
                    in_=sap.rearrange("(k p) n -> p k n", p=128)), dma=True)
            for tc in range(NTC):
                C.step("sync", lambda sy, tc=tc: sy.dma_start(
                    out=lntc[:].rearrange("p (k n) -> p k n", n=TC),
                    in_=d_ln1[:, tc * TC:(tc + 1) * TC].rearrange(
                        "(k p) n -> p k n", p=128)), dma=True)
                if e >= 0:
                    C.step("sync", lambda sy, e=e, tc=tc: sy.dma_start(
                        out=wrow[:], in_=w8T[e:e + 1, tc * TC:(tc + 1) * TC]), dma=True)
                    bcast_row(C, wrow[0:1, :], TC, 6)
                    C.step("vector", lambda v: v.tensor_copy(wbb[:], PSb(6, 0, TC)))
                for mt in range(NKI):
                    bnk = mt % 2
                    C.step("tensor", lambda t, mt=mt, bnk=bnk: _last(*[
                        t.matmul(PSb(bnk, 0, TC), WgT(kt, mt),
                                 lntc[:, kt * TC:(kt + 1) * TC],
                                 start=(kt == 0), stop=(kt == NPT - 1))
                        for kt in range(NPT)]))
                    C.step("scalar", lambda s, mt=mt, bnk=bnk: s.activation(
                        gu[:, mt * TC:(mt + 1) * TC], PSb(bnk, 0, TC), AF.Silu))
                    C.step("tensor", lambda t, mt=mt, bnk=bnk: _last(*[
                        t.matmul(PSb(2 + bnk, 0, TC), WuT(kt, mt),
                                 lntc[:, kt * TC:(kt + 1) * TC],
                                 start=(kt == 0), stop=(kt == NPT - 1))
                        for kt in range(NPT)]))
                    C.step("vector", lambda v, mt=mt, bnk=bnk: v.tensor_tensor(
                        gu[:, mt * TC:(mt + 1) * TC], gu[:, mt * TC:(mt + 1) * TC],
                        PSb(2 + bnk, 0, TC), ALU.mult))
                for mt in range(NPT):
                    bnk = 4 + mt % 2
                    C.step("tensor", lambda t, mt=mt, bnk=bnk: _last(*[
                        t.matmul(PSb(bnk, 0, TC), WdT(kt, mt),
                                 gu[:, kt * TC:(kt + 1) * TC],
                                 start=(kt == 0), stop=(kt == NKI - 1))
                        for kt in range(NKI)]))
                    if e < 0:
                        C.step("vector", lambda v, mt=mt, bnk=bnk: v.tensor_copy(
                            tmp5[:], PSb(bnk, 0, TC)))
                    else:
                        def v_acc(v, mt=mt, bnk=bnk):
                            return v.tensor_tensor(tmp5[:], PSb(bnk, 0, TC), wbb[:],
                                                   ALU.mult)
                        C.step("vector", v_acc)
                    C.step("gpsimd", lambda g, mt=mt, tc=tc: g.dma_start(
                        out=d_xres[:, mt * TP + tc * TC: mt * TP + (tc + 1) * TC],
                        in_=tmp5[:], accum_op=ALU.add), dma=True)
                C.flush()


def _emit_shift_mix(h, C, ln_sb, mix_pp_list, out_list, dtmp, tc):
    p = h["p"]
    def v_mix(v):
        for pt in range(NPT):
            base = pt * (TP + 1)
            cur = ln_sb[:, base + 1 + tc * TC: base + 1 + (tc + 1) * TC]
            prv = ln_sb[:, base + tc * TC: base + (tc + 1) * TC]
            i0 = v.tensor_tensor(dtmp[:], cur, prv, ALU.subtract)
            p.vg(v, i0)
            for mix_pp, out in zip(mix_pp_list, out_list):
                i1 = v.scalar_tensor_tensor(
                    out[:, pt * TC:(pt + 1) * TC], dtmp[:], mix_pp[:, pt:pt + 1], prv,
                    ALU.mult, ALU.add)
            if pt < NPT - 1:
                p.vg(v, i1)
        return i1
    C.step("vector", v_mix)


def _emit_ln_masked(h, C, d_xres, ln_sb, name):
    nc, p, blk = h["nc"], h["p"], h["blk"]
    C_, PSb = h["C"], h["PSb"]
    RTMP, m0, r0, epsA = h["RTMP"], h["m0"], h["r0"], h["epsA"]
    ln_stats_rows, bcast_row = h["ln_stats_rows"], h["bcast_row"]
    maskb = h["maskb"]
    with contextlib.ExitStack() as s0:
        xT = s0.enter_context(nc.sbuf_tensor("xT_" + name, [128, NPT * TP], F32))
        XT = lambda pt, tc: xT[:, pt * TP + tc * TC: pt * TP + (tc + 1) * TC]
        sq = s0.enter_context(nc.sbuf_tensor("sq_" + name, [128, TC], F32))
        C.step("sync", lambda sy: sy.dma_start(out=xT[:], in_=d_xres), dma=True)
        C.step("vector", lambda v: _last(*[
            v.memset(ln_sb[:, pt * (TP + 1): pt * (TP + 1) + 1], 0.0)
            for pt in range(NPT)]))
        for tc in range(NTC):
            ln_stats_rows(C, lambda pt, tc=tc: XT(pt, tc), NPT, TC, sq,
                          m0[:, tc * TC:(tc + 1) * TC], r0[:, tc * TC:(tc + 1) * TC],
                          RTMP[:, 0:TC], epsA, D)
            bcast_row(C, m0[:, tc * TC:(tc + 1) * TC], TC, 6)
            bcast_row(C, r0[:, tc * TC:(tc + 1) * TC], TC, 7)
            def v_lnm(v, tc=tc):
                for pt in range(NPT):
                    dst = ln_sb[:, pt * (TP + 1) + 1 + tc * TC: pt * (TP + 1) + 1 + (tc + 1) * TC]
                    i0 = v.tensor_tensor(sq[:], XT(pt, tc), PSb(6, 0, TC), ALU.subtract)
                    p.vg(v, i0)
                    i1 = v.tensor_tensor(sq[:], sq[:], PSb(7, 0, TC), ALU.mult)
                    p.vg(v, i1)
                    i2 = v.tensor_tensor(dst, sq[:], maskb[:, tc * TC:(tc + 1) * TC],
                                         ALU.mult)
                    if pt < NPT - 1:
                        p.vg(v, i2)
                return i2
            C.step("vector", v_lnm)
            C.flush()


def emit_stage_c(h):
    nc, p, blk, C, dram = h["nc"], h["p"], h["blk"], h["C"], h["dram"]
    PSb, sbuf = h["PSb"], h["sbuf"]
    d_xres = dram["xres"]
    d_kf, d_vf, d_sr, d_rw = (dram[k] for k in ("kf_scr", "vf_scr", "sr_scr", "rw_scr"))
    Wt = lambda w, kt, mt: w[:, kt * D + mt * 128: kt * D + (mt + 1) * 128]

    if "maskb" not in h:
        h["maskb"] = sbuf("maskb", [128, TP], F32)
        C.step("sync", lambda sy: sy.dma_start(out=h["maskb"][:], in_=dram["mask"]),
               dma=True)
    maskb = h["maskb"]
    vecsC = sbuf("vecsC", [128, 48], F32)
    for i, nm in enumerate(("l1_w", "l1_u", "l1_mk", "l1_mv", "l1_mr")):
        C.step("sync", lambda sy, i=i, nm=nm: sy.dma_start(
            out=vecsC[:, i * 8:(i + 1) * 8],
            in_=dram[nm].rearrange("(t p) -> p t", p=128),
            allow_slow_non_contiguous=True), dma=True)
    WV, UV, MK, MV, MR = (vecsC[:, i * 8:(i + 1) * 8] for i in range(5))
    DCOL = vecsC[:, 40:48]
    EU = UV

    ln2 = sbuf("ln2_sb", [128, NPT * (TP + 1)], BF16)
    _emit_ln_masked(h, C, d_xres, ln2, "c")

    def s_wexp(s):
        s.activation(DCOL[:, :], WV[:, :], AF.Exp)
        return s.activation(vecsC[:, 8:16], UV[:, :], AF.Exp)
    C.step("scalar", s_wexp)
    C.step("vector", lambda v: v.tensor_scalar_mul(DCOL[:, :], DCOL[:, :], -1.0))
    C.step("scalar", lambda s: s.activation(DCOL[:, :], DCOL[:, :], AF.Exp))

    with contextlib.ExitStack() as s1:
        wk = s1.enter_context(nc.sbuf_tensor("wkC", [128, NPT * D], BF16))
        wv = s1.enter_context(nc.sbuf_tensor("wvC", [128, NPT * D], BF16))
        wr = s1.enter_context(nc.sbuf_tensor("wrC", [128, NPT * D], BF16))
        xk = s1.enter_context(nc.sbuf_tensor("xkC", [128, NPT * TC], BF16))
        xv = s1.enter_context(nc.sbuf_tensor("xvC", [128, NPT * TC], BF16))
        xr = s1.enter_context(nc.sbuf_tensor("xrC", [128, NPT * TC], BF16))
        dtmp = s1.enter_context(nc.sbuf_tensor("dtmpC", [128, TC], F32))
        kst = s1.enter_context(nc.sbuf_tensor("kstC", [128, TC], F32))
        srt = s1.enter_context(nc.sbuf_tensor("srtC", [128, TC], BF16))
        for dst, nm in ((wk, "Wk_T"), (wv, "Wv_T"), (wr, "Wr_T")):
            C.step("gpsimd", lambda g, dst=dst, nm=nm: g.dma_start(
                out=dst[:].rearrange("p (k n) -> p k n", n=D),
                in_=dram[nm].rearrange("(k p) n -> p k n", p=128)), dma=True)
        for tc in range(NTC):
            _emit_shift_mix(h, C, ln2, [MK, MV, MR], [xk, xv, xr], dtmp, tc)
            for mt in range(NPT):
                C.step("tensor", lambda t, mt=mt: _last(*[
                    t.matmul(PSb(mt % 2, 0, TC), Wt(wk, kt, mt),
                             xk[:, kt * TC:(kt + 1) * TC],
                             start=(kt == 0), stop=(kt == NPT - 1)) for kt in range(NPT)]))
                C.step("vector", lambda v, mt=mt: v.tensor_copy(kst[:], PSb(mt % 2, 0, TC)))
                C.step("sync", lambda sy, mt=mt, tc=tc: sy.dma_start(
                    out=d_kf[mt * 128:(mt + 1) * 128, tc * TC:(tc + 1) * TC],
                    in_=kst[:]), dma=True)
                C.step("tensor", lambda t, mt=mt: _last(*[
                    t.matmul(PSb(2 + mt % 2, 0, TC), Wt(wv, kt, mt),
                             xv[:, kt * TC:(kt + 1) * TC],
                             start=(kt == 0), stop=(kt == NPT - 1)) for kt in range(NPT)]))
                C.step("vector", lambda v, mt=mt: v.tensor_copy(kst[:], PSb(2 + mt % 2, 0, TC)))
                C.step("sync", lambda sy, mt=mt, tc=tc: sy.dma_start(
                    out=d_vf[mt * 128:(mt + 1) * 128, tc * TC:(tc + 1) * TC],
                    in_=kst[:]), dma=True)
                C.step("tensor", lambda t, mt=mt: _last(*[
                    t.matmul(PSb(4 + mt % 2, 0, TC), Wt(wr, kt, mt),
                             xr[:, kt * TC:(kt + 1) * TC],
                             start=(kt == 0), stop=(kt == NPT - 1)) for kt in range(NPT)]))
                C.step("scalar", lambda s, mt=mt: s.activation(
                    srt[:], PSb(4 + mt % 2, 0, TC), AF.Sigmoid))
                C.step("sync", lambda sy, mt=mt, tc=tc: sy.dma_start(
                    out=d_sr[mt * 128:(mt + 1) * 128, tc * TC:(tc + 1) * TC],
                    in_=srt[:]), dma=True)
            C.flush()

    with contextlib.ExitStack() as s2:
        kf = s2.enter_context(nc.sbuf_tensor("kfC", [128, TP], F32))
        vf = s2.enter_context(nc.sbuf_tensor("vfC", [128, TP], F32))
        af = s2.enter_context(nc.sbuf_tensor("afC", [128, TP], F32))
        avf = s2.enter_context(nc.sbuf_tensor("avC", [128, TP], F32))
        Pf = s2.enter_context(nc.sbuf_tensor("PfC", [128, TP + 1], F32))
        Qf = s2.enter_context(nc.sbuf_tensor("QfC", [128, TP + 1], F32))
        dtl = s2.enter_context(nc.sbuf_tensor("dtlC", [128, TP], F32))
        srf = s2.enter_context(nc.sbuf_tensor("srfC", [128, TP], BF16))
        rwt = s2.enter_context(nc.sbuf_tensor("rwtC", [128, TP], BF16))
        for pt in range(NPT):
            C.step("sync", lambda sy, pt=pt: sy.dma_start(
                out=kf[:], in_=d_kf[pt * 128:(pt + 1) * 128, :]), dma=True)
            C.step("sync", lambda sy, pt=pt: sy.dma_start(
                out=vf[:], in_=d_vf[pt * 128:(pt + 1) * 128, :]), dma=True)
            C.step("sync", lambda sy, pt=pt: sy.dma_start(
                out=srf[:], in_=d_sr[pt * 128:(pt + 1) * 128, :]), dma=True)
            def v_pre(v, pt=pt):
                i0 = v.memset(dtl[:], 1.0)
                p.vg(v, i0)
                i1 = v.tensor_scalar_mul(dtl[:], dtl[:], DCOL[:, pt:pt + 1])
                i2 = v.memset(Pf[:, 0:1], 0.0)
                return v.memset(Qf[:, 0:1], 0.0)
            C.step("vector", v_pre)
            C.step("scalar", lambda s: s.activation(af[:], kf[:], AF.Exp))
            def v_scan(v, pt=pt):
                i0 = v.tensor_tensor(af[:], af[:], maskb[:], ALU.mult)
                p.vg(v, i0)
                i1 = v.tensor_tensor(avf[:], af[:], vf[:], ALU.mult)
                p.vg(v, i1)
                i2 = v.tensor_tensor_scan(Pf[:, 1:TP + 1], dtl[:], avf[:], 0.0,
                                          ALU.mult, ALU.add)
                p.vg(v, i2)
                i3 = v.tensor_tensor_scan(Qf[:, 1:TP + 1], dtl[:], af[:], 0.0,
                                          ALU.mult, ALU.add)
                p.vg(v, i3)
                i4 = v.scalar_tensor_tensor(avf[:], avf[:], vecsC[:, 8 + pt:9 + pt],
                                            Pf[:, 0:TP], ALU.mult, ALU.add)
                p.vg(v, i4)
                i5 = v.scalar_tensor_tensor(af[:], af[:], vecsC[:, 8 + pt:9 + pt],
                                            Qf[:, 0:TP], ALU.mult, ALU.add)
                p.vg(v, i5)
                i6 = v.tensor_scalar_max(af[:], af[:], 1e-30)
                p.vg(v, i6)
                i7 = v.reciprocal(kf[:], af[:])
                p.vg(v, i7)
                i8 = v.tensor_tensor(avf[:], avf[:], kf[:], ALU.mult)
                p.vg(v, i8)
                return v.tensor_tensor(rwt[:], avf[:], srf[:], ALU.mult)
            C.step("vector", v_scan)
            C.step("sync", lambda sy, pt=pt: sy.dma_start(
                out=d_rw[pt * 128:(pt + 1) * 128, :], in_=rwt[:]), dma=True)
            C.flush()

    with contextlib.ExitStack() as s3:
        wo = s3.enter_context(nc.sbuf_tensor("woC", [128, NPT * D], BF16))
        rwc = s3.enter_context(nc.sbuf_tensor("rwcC", [128, NPT * TC], BF16))
        tmp6 = s3.enter_context(nc.sbuf_tensor("tmp6C", [128, TC], F32))
        C.step("gpsimd", lambda g: g.dma_start(
            out=wo[:].rearrange("p (k n) -> p k n", n=D),
            in_=dram["Wo_T"].rearrange("(k p) n -> p k n", p=128)), dma=True)
        for tc in range(NTC):
            C.step("sync", lambda sy, tc=tc: sy.dma_start(
                out=rwc[:].rearrange("p (k n) -> p k n", n=TC),
                in_=d_rw[:, tc * TC:(tc + 1) * TC].rearrange("(k p) n -> p k n", p=128)),
                dma=True)
            for mt in range(NPT):
                C.step("tensor", lambda t, mt=mt: _last(*[
                    t.matmul(PSb(mt % 2, 0, TC), Wt(wo, kt, mt),
                             rwc[:, kt * TC:(kt + 1) * TC],
                             start=(kt == 0), stop=(kt == NPT - 1)) for kt in range(NPT)]))
                C.step("vector", lambda v, mt=mt: v.tensor_copy(tmp6[:], PSb(mt % 2, 0, TC)))
                C.step("gpsimd", lambda g, mt=mt, tc=tc: g.dma_start(
                    out=d_xres[:, mt * TP + tc * TC: mt * TP + (tc + 1) * TC],
                    in_=tmp6[:], accum_op=ALU.add), dma=True)
            C.flush()


def emit_stage_d(h):
    nc, p, blk, C, dram = h["nc"], h["p"], h["blk"], h["C"], h["dram"]
    PSb, sbuf = h["PSb"], h["sbuf"]
    d_xres = dram["xres"]
    Wt = lambda w, kt, mt: w[:, kt * D + mt * 128: kt * D + (mt + 1) * 128]

    if "maskb" not in h:
        h["maskb"] = sbuf("maskb", [128, TP], F32)
        C.step("sync", lambda sy: sy.dma_start(out=h["maskb"][:], in_=dram["mask"]),
               dma=True)
    vecsD = sbuf("vecsD", [128, 16], F32)
    for i, nm in enumerate(("f_mur", "f_muk")):
        C.step("sync", lambda sy, i=i, nm=nm: sy.dma_start(
            out=vecsD[:, i * 8:(i + 1) * 8],
            in_=dram[nm].rearrange("(t p) -> p t", p=128),
            allow_slow_non_contiguous=True), dma=True)
    MUR, MUK = vecsD[:, 0:8], vecsD[:, 8:16]

    ln3 = sbuf("ln3_sb", [128, NPT * (TP + 1)], BF16)
    _emit_ln_masked(h, C, d_xres, ln3, "d")

    with contextlib.ExitStack() as s1:
        wrD = s1.enter_context(nc.sbuf_tensor("wrD", [128, NPT * D], BF16))
        wkD = s1.enter_context(nc.sbuf_tensor("wkD", [128, NPT * D], BF16))
        wvD = s1.enter_context(nc.sbuf_tensor("wvD", [128, NPT * D], BF16))
        xrD = s1.enter_context(nc.sbuf_tensor("xrD", [128, NPT * TC], BF16))
        xkD = s1.enter_context(nc.sbuf_tensor("xkD", [128, NPT * TC], BF16))
        dtmp = s1.enter_context(nc.sbuf_tensor("dtmpD", [128, TC], F32))
        srD = s1.enter_context(nc.sbuf_tensor("srD", [128, NPT * TC], BF16))
        kaD = s1.enter_context(nc.sbuf_tensor("kaD", [128, NPT * TC], BF16))
        tmpD = s1.enter_context(nc.sbuf_tensor("tmpD", [128, TC], F32))
        for dst, nm in ((wrD, "f_Wr_T"), (wkD, "f_Wk_T"), (wvD, "f_Wv_T")):
            C.step("gpsimd", lambda g, dst=dst, nm=nm: g.dma_start(
                out=dst[:].rearrange("p (k n) -> p k n", n=D),
                in_=dram[nm].rearrange("(k p) n -> p k n", p=128)), dma=True)
        for tc in range(NTC):
            _emit_shift_mix(h, C, ln3, [MUR, MUK], [xrD, xkD], dtmp, tc)
            for mt in range(NPT):
                C.step("tensor", lambda t, mt=mt: _last(*[
                    t.matmul(PSb(mt % 2, 0, TC), Wt(wrD, kt, mt),
                             xrD[:, kt * TC:(kt + 1) * TC],
                             start=(kt == 0), stop=(kt == NPT - 1)) for kt in range(NPT)]))
                C.step("scalar", lambda s, mt=mt: s.activation(
                    srD[:, mt * TC:(mt + 1) * TC], PSb(mt % 2, 0, TC), AF.Sigmoid))
                C.step("tensor", lambda t, mt=mt: _last(*[
                    t.matmul(PSb(2 + mt % 2, 0, TC), Wt(wkD, kt, mt),
                             xkD[:, kt * TC:(kt + 1) * TC],
                             start=(kt == 0), stop=(kt == NPT - 1)) for kt in range(NPT)]))
                def v_ka(v, mt=mt):
                    i0 = v.tensor_scalar_max(tmpD[:], PSb(2 + mt % 2, 0, TC), 0.0)
                    p.vg(v, i0)
                    return v.tensor_tensor(kaD[:, mt * TC:(mt + 1) * TC],
                                           PSb(2 + mt % 2, 0, TC), tmpD[:], ALU.mult)
                C.step("vector", v_ka)
            for mt in range(NPT):
                C.step("tensor", lambda t, mt=mt: _last(*[
                    t.matmul(PSb(4 + mt % 2, 0, TC), Wt(wvD, kt, mt),
                             kaD[:, kt * TC:(kt + 1) * TC],
                             start=(kt == 0), stop=(kt == NPT - 1)) for kt in range(NPT)]))
                def v_out(v, mt=mt):
                    return v.tensor_tensor(tmpD[:], PSb(4 + mt % 2, 0, TC),
                                           srD[:, mt * TC:(mt + 1) * TC], ALU.mult)
                C.step("vector", v_out)
                C.step("gpsimd", lambda g, mt=mt, tc=tc: g.dma_start(
                    out=d_xres[:, mt * TP + tc * TC: mt * TP + (tc + 1) * TC],
                    in_=tmpD[:], accum_op=ALU.add), dma=True)
            C.flush()


def emit_output(h):
    nc, p, blk, C, dram = h["nc"], h["p"], h["blk"], h["C"], h["dram"]
    PSb, idf = h["PSb"], h["idf"]
    with contextlib.ExitStack() as s0:
        xT = s0.enter_context(nc.sbuf_tensor("xT_out", [128, NPT * TP], F32))
        yst = s0.enter_context(nc.sbuf_tensor("yst", [128, D], F32))
        C.step("sync", lambda sy: sy.dma_start(out=xT[:], in_=dram["xres"]), dma=True)
        for tt in range(CH // 128):
            def t_tp(t, tt=tt):
                for pt in range(NPT):
                    ins = t.transpose(PSb(pt // 2, (pt % 2) * 256 + 0, 128)
                                      if False else
                                      PSb(pt // 4, (pt % 4) * 128, 128),
                                      xT[:, pt * TP + W + tt * 128: pt * TP + W + (tt + 1) * 128],
                                      idf[:])
                return ins
            C.step("tensor", t_tp)
            def v_cp(v, tt=tt):
                for pt in range(NPT):
                    ins = v.tensor_copy(yst[:, pt * 128:(pt + 1) * 128],
                                        PSb(pt // 4, (pt % 4) * 128, 128))
                return ins
            C.step("vector", v_cp)
            C.step("sync", lambda sy, tt=tt: sy.dma_start(
                out=dram["y"][tt * 128:(tt + 1) * 128, :], in_=yst[:]), dma=True)
        C.flush()


def emit_attention(h, d_kT, d_vh, d_qT, d_oT):
    nc, p, blk, C = h["nc"], h["p"], h["blk"], h["C"]
    PSb, PSf, onesrf = h["PSb"], h["PSf"], h["onesrf"]
    with contextlib.ExitStack() as ph:
        ksb = ph.enter_context(nc.sbuf_tensor("ksb", [128, S], BF16))
        vsb = ph.enter_context(nc.sbuf_tensor("vsb", [128, NKT * 130], BF16))
        qhp = ph.enter_context(nc.sbuf_tensor("qhp", [128, TP], BF16))
        ohp = ph.enter_context(nc.sbuf_tensor("ohp", [128, TP], BF16))
        pb = ph.enter_context(nc.sbuf_tensor("pb", [128, 3 * 2 * TC], BF16))
        recr = ph.enter_context(nc.sbuf_tensor("recr", [1, 2 * TC], F32))
        brs = ph.enter_context(nc.sbuf_tensor("brs", [128, 2 * TC], F32))
        sqk, sxp, spv, skl, sqn, srec, sbrr, sost = (p.sem(x) for x in
                                                     ("sqk", "sxp", "spv", "skl", "sqn",
                                                      "srec", "sbrr", "sost"))
        C.flush()
        tl = C.tail()

        @blk.sync
        def _(sy):
            if tl:
                sy.wait_ge(p.sem(tl[0]), tl[1])
            for hp in range(NHP):
                if hp >= 1:
                    sy.wait_ge(spv, hp * NTC * NKT)
                    sy.wait_ge(sost, hp * 16)
                sy.dma_start(out=ksb[:], in_=d_kT[hp * 128:(hp + 1) * 128, :]).then_inc(skl, 16)
                sy.dma_start(out=vsb[:].rearrange("p (t c) -> p t c", c=130),
                             in_=d_vh[hp].rearrange("(t p) c -> p t c", p=128)
                             ).then_inc(skl, 16)
                sy.dma_start(out=qhp[:], in_=d_qT[hp * 128:(hp + 1) * 128, :]).then_inc(skl, 16)
                sy.wait_ge(sqn, (hp + 1) * NTC)
                sy.dma_start(out=d_oT[hp * 128:(hp + 1) * 128, :], in_=ohp[:]).then_inc(sost, 16)

        @blk.tensor
        def _(t):
            if tl:
                t.wait_ge(p.sem(tl[0]), tl[1])
            nqk = 0
            for hp in range(NHP):
                t.wait_ge(skl, (hp + 1) * 48)
                for qc in range(NTC):
                    it = hp * NTC + qc
                    for kt in range(NKT):
                        slot = kt % 3
                        if nqk >= 3:
                            t.wait_ge(sxp, nqk - 2)
                        if kt == 0 and it > 0:
                            t.wait_ge(sqn, it)
                        t.matmul(PSb(2 * slot, 0, TC),
                                 ksb[0:64, kt * 128:(kt + 1) * 128],
                                 qhp[0:64, qc * TC:(qc + 1) * TC],
                                 start=True, stop=True, skip_group_check=True)
                        t.matmul(PSb(2 * slot + 1, 0, TC),
                                 ksb[64:128, kt * 128:(kt + 1) * 128],
                                 qhp[64:128, qc * TC:(qc + 1) * TC],
                                 start=True, stop=True, tile_position=(64, 0),
                                 skip_group_check=True).then_inc(sqk, 1)
                        nqk += 1
                        t.wait_ge(sxp, nqk)
                        t.matmul(PSb(6, 0, TC)[0:65, :],
                                 vsb[:, kt * 130: kt * 130 + 65],
                                 pb[:, (slot * 2) * TC:(slot * 2 + 1) * TC],
                                 start=(kt == 0), stop=(kt == NKT - 1),
                                 skip_group_check=True)
                        t.matmul(PSb(6, 0, TC)[96:97, :],
                                 vsb[:, kt * 130 + 64: kt * 130 + 65],
                                 pb[:, (slot * 2 + 1) * TC:(slot * 2 + 2) * TC],
                                 start=(kt == 0), stop=(kt == NKT - 1),
                                 tile_position=(0, 96), skip_group_check=True)
                        t.matmul(PSb(7, 0, TC)[64:128, :],
                                 vsb[:, kt * 130 + 65: kt * 130 + 129],
                                 pb[:, (slot * 2 + 1) * TC:(slot * 2 + 2) * TC],
                                 start=(kt == 0), stop=(kt == NKT - 1),
                                 tile_position=(0, 64),
                                 skip_group_check=True).then_inc(spv, 1)
                    t.wait_ge(srec, it + 1)
                    t.matmul(PSb(0, 0, TC), onesrf[:], recr[0:1, 0:TC],
                             start=True, stop=True, skip_group_check=True)
                    t.matmul(PSb(1, 0, TC), onesrf[:], recr[0:1, TC:2 * TC],
                             start=True, stop=True,
                             skip_group_check=True).then_inc(sbrr, 1)

        @blk.scalar
        def _(s):
            nxp = 0
            for hp in range(NHP):
                for qc in range(NTC):
                    for kt in range(NKT):
                        slot = kt % 3
                        s.wait_ge(sqk, nxp + 1)
                        if nxp >= 3:
                            s.wait_ge(spv, nxp - 2)
                        s.activation(
                            pb[:, slot * 2 * TC:(slot + 1) * 2 * TC].rearrange(
                                "p (b n) -> p b n", b=2),
                            PSf[:, 2 * slot * BANK:(2 * slot + 2) * BANK].rearrange(
                                "p (b n) -> p b n", b=2)[:, :, 0:TC],
                            AF.Exp, scale=0.125).then_inc(sxp, 1)
                        nxp += 1

        @blk.vector
        def _(v):
            for hp in range(NHP):
                for qc in range(NTC):
                    it = hp * NTC + qc
                    if qc == 0 and hp >= 1:
                        v.wait_ge(sost, hp * 16)
                    v.wait_ge(spv, (it + 1) * NKT)
                    v.reciprocal(recr[0:1, 0:TC], PSb(6, 0, TC)[64:65, :])
                    v.reciprocal(recr[0:1, TC:2 * TC],
                                 PSb(6, 0, TC)[96:97, :]).then_inc(srec, 1)
                    v.wait_ge(sbrr, it + 1)
                    ia = v.tensor_copy(brs[0:64, 0:TC], PSb(0, 0, TC)[0:64, :])
                    ib = v.tensor_copy(brs[64:128, TC:2 * TC], PSb(1, 0, TC)[64:128, :])
                    p.vg(v, ia)
                    p.vg(v, ib)
                    v.tensor_tensor(ohp[0:64, qc * TC:(qc + 1) * TC],
                                    PSb(6, 0, TC)[0:64, :], brs[0:64, 0:TC], ALU.mult)
                    v.tensor_tensor(ohp[64:128, qc * TC:(qc + 1) * TC],
                                    PSb(7, 0, TC)[64:128, :], brs[64:128, TC:2 * TC],
                                    ALU.mult).then_inc(sqn, 1)

        C.step("vector", lambda v: _last(v.wait_ge(sost, NHP * 16),
                                         v.tensor_copy(brs[0:1, 0:1], brs[0:1, 0:1])))
        C.flush()



def _prep_core_inputs(c, full):
    b, j = c // 4, c % 4
    x = full["L0_attn_x"][b]
    lo = j * CH - W
    if lo < 0:
        xs = np.concatenate([np.zeros((W, D), np.float32), np.asarray(x[0:CH])], 0)
        mask = np.concatenate([np.zeros((128, W), np.float32),
                               np.ones((128, CH), np.float32)], 1)
    else:
        xs = np.ascontiguousarray(x[lo: (j + 1) * CH])
        mask = np.ones((128, TP), np.float32)
    T = lambda a: np.ascontiguousarray(np.asarray(a).T)
    A = lambda a: np.ascontiguousarray(np.asarray(a))
    return {
        "xs": np.ascontiguousarray(xs),
        "xb": A(x),
        "mask": mask,
        "ident": np.eye(128, dtype=np.float32),
        "w_q": A(full["L0_attn_w_q"]),
        "w_k": A(full["L0_attn_w_k"]),
        "w_v": A(full["L0_attn_w_v"]),
        "w_proj": A(full["L0_attn_w_proj"]),
        "b_proj": A(full["L0_attn_b_proj"]),
        "ln_g": A(full["L0_attn_ln_gamma"]),
        "ln_b": A(full["L0_attn_ln_beta"]),
        "sh_gate_T": T(full["L0_ffn_shared_gate"]),
        "sh_up_T": T(full["L0_ffn_shared_up"]),
        "sh_down_T": T(full["L0_ffn_shared_down"]),
        "gate_w": A(full["L0_ffn_gate_w"]),
        "up_w": A(full["L0_ffn_up_w"]),
        "down_w": A(full["L0_ffn_down_w"]),
        "router_T": T(full["L0_ffn_router_w"]),
        "r_bias": A(full["L0_ffn_routing_bias"]),
        "l1_w": A(full["L1_attn_w"]),
        "l1_u": A(full["L1_attn_u"]),
        "l1_mk": A(full["L1_attn_mix_k"]),
        "l1_mv": A(full["L1_attn_mix_v"]),
        "l1_mr": A(full["L1_attn_mix_r"]),
        "Wk_T": T(full["L1_attn_Wk"]),
        "Wv_T": T(full["L1_attn_Wv"]),
        "Wr_T": T(full["L1_attn_Wr"]),
        "Wo_T": T(full["L1_attn_Wo"]),
        "f_Wr_T": T(full["L1_ffn_W_r"]),
        "f_Wk_T": T(full["L1_ffn_W_k"]),
        "f_Wv_T": T(full["L1_ffn_W_v"]),
        "f_mur": A(full["L1_ffn_mu_r"]),
        "f_muk": A(full["L1_ffn_mu_k"]),
    }


ALL_STAGES = ("IN", "A", "B", "C", "D", "OUT")


def kernel(**inputs):
    nc, out_names = build_program(stages=ALL_STAGES)
    in_maps = [_prep_core_inputs(c, inputs) for c in range(8)]
    res = run_bass_kernel_spmd(nc, in_maps, core_ids=list(range(8)))
    out = np.zeros((B, S, D), np.float32)
    for c in range(8):
        b, j = c // 4, c % 4
        out[b, j * CH:(j + 1) * CH, :] = res.results[c]["y"]
    return out


# version 03
# speedup vs baseline: 1.5832x; 1.5832x over previous
import contextlib
import numpy as np
import ml_dtypes
import concourse.bass as bass
import concourse.mybir as mybir
from concourse.bass_utils import run_bass_kernel_spmd

F32 = mybir.dt.float32
BF16 = mybir.dt.bfloat16
AF = mybir.ActivationFunctionType
ALU = mybir.AluOpType

B, S, D = 2, 8192, 1024
W = 256
CH = 2048
TP = CH + W
TC = 384
NTC = TP // TC
NPT = D // 128
NTT = TP // 128
BCH = 512
NBC = S // BCH
NHP = 8
NKT = S // 128
HD = 64
I_FF = 2048
NPI = I_FF // 128
NE = 8
EPS_OUT = 1e-5
EPS_IN = 1e-6
BANK = 512


class Prog:
    def __init__(self, nc, block, es):
        self.nc, self.block, self.es = nc, block, es
        self._sems = {}

    def sem(self, name):
        if name not in self._sems:
            self._sems[name] = self.es.enter_context(self.nc.semaphore(name))
        return self._sems[name]

    def vg(self, v, inst):
        self._vch = getattr(self, "_vch", 0) + 1
        inst.then_inc(self.sem("vch"), 1)
        v.wait_ge(self.sem("vch"), self._vch)


class Chain:

    def __init__(self, p, tag):
        self.p = p
        self.tag = tag
        self.cums = {}
        self.steps = []
        self.last = None

    def _bump(self, name, amt):
        v = self.cums.get(name, 0) + amt
        self.cums[name] = v
        return v

    def step(self, eng, fn, dma=False):
        semname = (self.tag + "_D" + eng) if dma else self.tag
        amt = 16 if dma else 1
        v = self._bump(semname, amt)
        self.steps.append((eng, fn, semname, amt, self.last, dma))
        self.last = (semname, v)

    def tail(self):
        return self.last

    def flush(self):
        per_eng = {}
        for (eng, fn, semname, amt, wait, dma) in self.steps:
            per_eng.setdefault(eng, []).append((wait, fn, semname, amt))
        for eng_name, items in per_eng.items():
            def section(e, items=items):
                for wait, fn, semname, amt in items:
                    if wait is not None:
                        e.wait_ge(self.p.sem(wait[0]), wait[1])
                    fn(e).then_inc(self.p.sem(semname), amt)
            getattr(self.p.block, eng_name)(section)
        self.steps = []


def _last(*insts):
    return insts[-1]


def build_program(stages=("A",), dbg_out=None):
    nc = bass.Bass("TRN2", target_bir_lowering=False)
    es = contextlib.ExitStack()

    dram = {}
    def din(name, shape, dt=F32):
        dram[name] = nc.dram_tensor(name, shape, dt, kind="ExternalInput").ap()
        return dram[name]
    def dint(name, shape, dt):
        dram[name] = nc.dram_tensor(name, shape, dt, kind="Internal").ap()
        return dram[name]

    din("ident", [128, 128])
    dint("xres", [128, NPT * TP], F32)
    if "IN" in stages:
        din("xs", [TP, D])
    else:
        din("xres_in", [128, NPT * TP], F32)
    if "C" in stages or "D" in stages:
        din("mask", [128, TP])
    if "C" in stages:
        din("l1_w", [D]); din("l1_u", [D]); din("l1_mk", [D]); din("l1_mv", [D]); din("l1_mr", [D])
        din("Wk_T", [D, D]); din("Wv_T", [D, D]); din("Wr_T", [D, D]); din("Wo_T", [D, D])
        dint("kf_scr", [D, TP], F32)
        dint("vf_scr", [D, TP], F32)
        dint("sr_scr", [D, TP], BF16)
        dint("rw_scr", [D, TP], BF16)
    if "D" in stages:
        din("f_Wr_T", [D, D]); din("f_Wk_T", [D, D]); din("f_Wv_T", [D, D])
        din("f_mur", [D]); din("f_muk", [D])
    if "B" in stages:
        din("sh_gate_T", [D, I_FF]); din("sh_up_T", [D, I_FF]); din("sh_down_T", [I_FF, D])
        din("gate_w", [NE, D, I_FF]); din("up_w", [NE, D, I_FF]); din("down_w", [NE, I_FF, D])
        din("router_T", [D, NE]); din("r_bias", [NE])
        dint("ln1_scr", [D, TP], BF16)
    if "A" in stages:
        din("xb", [S, D])
        din("w_q", [D, D]); din("w_k", [D, D]); din("w_v", [D, D]); din("w_proj", [D, D])
        din("b_proj", [D]); din("ln_g", [D]); din("ln_b", [D])
        dint("kT_scr", [D, S], BF16)
        dint("v_scr", [NHP, S, 130], BF16)
        dint("qT_scr", [D, TP], BF16)
        dint("oT_scr", [D, TP], BF16)

    out_names = []
    if "OUT" in stages:
        dram["y"] = nc.dram_tensor("y", [CH, D], F32, kind="ExternalOutput").ap()
        out_names.append("y")
    if dbg_out:
        dram[dbg_out] = nc.dram_tensor(dbg_out, [128, NPT * TP], F32,
                                       kind="ExternalOutput").ap()
        out_names.append(dbg_out)

    with es:
        blk = es.enter_context(nc.Block())
        p = Prog(nc, blk, es)
        sbuf = lambda nm, shp, dt: es.enter_context(nc.sbuf_tensor(nm, shp, dt))

        h = {"dram": dram}
        h["ones1f"] = ones1f = sbuf("ones1f", [128, 1], F32)
        h["onesrf"] = onesrf = sbuf("onesrf", [1, 128], F32)
        h["idf"] = idf = sbuf("idf", [128, 128], F32)
        h["idb"] = idb = sbuf("idb", [128, 128], BF16)
        h["epsA"] = epsA = sbuf("epsA", [1, 1], F32)
        h["epsB"] = epsB = sbuf("epsB", [1, 1], F32)
        rows = sbuf("rows", [1, TP + 2 * BCH], F32)
        h["RTMP"] = rows[0:1, 0:TP]
        h["RB0"] = rows[0:1, TP:TP + BCH]
        h["RB1"] = rows[0:1, TP + BCH:TP + 2 * BCH]
        h["m0"] = sbuf("m0", [1, TP], F32)
        h["r0"] = sbuf("r0", [1, TP], F32)
        PSf = es.enter_context(nc.psum_tensor("PSf", [128, 8 * BANK], F32))
        h["PSb"] = PSb = lambda k, off=0, n=BANK: PSf[:, k * BANK + off: k * BANK + off + n]
        h["PSf"] = PSf
        h["sbuf"] = sbuf
        h["nc"], h["p"], h["blk"] = nc, p, blk

        C = Chain(p, "main")
        h["C"] = C

        self_p = p
        def ln_stats_rows(ch, src_tiles, npt, n, sq_buf, mrow, rrow, trow, eps_ap, nelem):
            for pt in range(npt):
                ch.step("tensor", lambda t, pt=pt: t.matmul(
                    PSb(4, 0, n)[0:1, :], ones1f[:], src_tiles(pt),
                    start=(pt == 0), stop=(pt == npt - 1)))
                ch.step("scalar", lambda s, pt=pt: s.activation(
                    sq_buf[:, 0:n], src_tiles(pt), AF.Square))
                ch.step("tensor", lambda t, pt=pt: t.matmul(
                    PSb(5, 0, n)[0:1, :], ones1f[:], sq_buf[:, 0:n],
                    start=(pt == 0), stop=(pt == npt - 1)))
            def v_rows(v):
                i1 = v.tensor_scalar_mul(mrow, PSb(4, 0, n)[0:1, :], 1.0 / nelem)
                v.tensor_scalar_mul(trow, PSb(5, 0, n)[0:1, :], 1.0 / nelem)
                self_p.vg(v, i1)
                i2 = v.tensor_tensor(rrow, mrow, mrow, ALU.mult)
                self_p.vg(v, i2)
                return v.tensor_tensor(trow, trow, rrow, ALU.subtract)
            ch.step("vector", v_rows)
            ch.step("scalar", lambda s: s.activation(trow, trow, AF.Sqrt, bias=eps_ap[:, 0:1]))
            ch.step("vector", lambda v: v.reciprocal(rrow, trow))
        h["ln_stats_rows"] = ln_stats_rows

        def bcast_row(ch, row_ap, n, bank):
            ch.step("tensor", lambda t: t.matmul(PSb(bank, 0, n), onesrf[:], row_ap,
                                                 start=True, stop=True))
        h["bcast_row"] = bcast_row

        def normalize(ch, src_tiles, dst_tiles, npt, n, mb, rb):
            def v_norm(v):
                for pt in range(npt):
                    d = dst_tiles(pt)
                    i0 = v.tensor_tensor(d, src_tiles(pt), mb, ALU.subtract)
                    self_p.vg(v, i0)
                    i1 = v.tensor_tensor(d, d, rb, ALU.mult)
                return i1
            ch.step("vector", v_norm)
        h["normalize"] = normalize

        def mm_acc(t, psum_ap, lhsT_fn, rhs_fn, nk):
            for kt in range(nk):
                i1 = t.matmul(psum_ap, lhsT_fn(kt), rhs_fn(kt),
                              start=(kt == 0), stop=(kt == nk - 1))
            return i1
        h["mm_acc"] = mm_acc

        C.step("vector", lambda v: _last(
            v.memset(ones1f[:], 1.0), v.memset(onesrf[:], 1.0),
            v.memset(epsA[:], EPS_OUT), v.memset(epsB[:], EPS_IN)))
        C.step("sync", lambda sy: sy.dma_start(out=idf[:], in_=dram["ident"]), dma=True)
        C.step("gpsimd", lambda g: g.dma_start(out=idb[:], in_=dram["ident"]), dma=True)
        C.flush()

        if "IN" not in stages:
            C.step("sync", lambda sy: sy.dma_start(out=dram["xres"],
                                                   in_=dram["xres_in"]), dma=True)
        if "A" in stages:
            emit_stage_a(h)
        if "B" in stages:
            emit_stage_b(h)
        if "C" in stages:
            emit_stage_c(h)
        if "D" in stages:
            emit_stage_d(h)
        if "OUT" in stages:
            emit_output(h)

        if dbg_out:
            C.flush()
            tl = C.tail()
            st = p.sem("st_dbg")

            @blk.sync
            def _(sy):
                if tl:
                    sy.wait_ge(p.sem(tl[0]), tl[1])
                sy.dma_start(out=dram[dbg_out], in_=dram["xres"]).then_inc(st, 16)
                sy.wait_ge(st, 16)

        C.flush()
        finals = dict(C.cums)

        @blk.sync
        def _(sy):
            for name, val in finals.items():
                sy.wait_ge(p.sem(name), val)
    return nc, out_names


def emit_transpose_in(h, xT):
    nc, p, blk, C = h["nc"], h["p"], h["blk"], h["C"]
    PSb, idf = h["PSb"], h["idf"]
    d_src = h["dram"]["xs"]
    with contextlib.ExitStack() as ph:
        xstage = ph.enter_context(nc.sbuf_tensor("xstage", [128, 2 * D], F32))
        xld = [p.sem("xld0"), p.sem("xld1")]
        xtp, xcp = p.sem("xtp"), p.sem("xcp")
        C.flush()
        tl = C.tail()

        @blk.sync
        def _(sy):
            if tl:
                sy.wait_ge(p.sem(tl[0]), tl[1])
            for tt in range(NTT):
                if tt >= 2:
                    sy.wait_ge(xcp, tt - 1)
                sy.dma_start(out=xstage[:, (tt % 2) * D:(tt % 2 + 1) * D],
                             in_=d_src[tt * 128:(tt + 1) * 128, :]).then_inc(xld[tt % 2], 16)

        @blk.tensor
        def _(t):
            if tl:
                t.wait_ge(p.sem(tl[0]), tl[1])
            for tt in range(NTT):
                t.wait_ge(xld[tt % 2], (tt // 2 + 1) * 16)
                if tt >= 2:
                    t.wait_ge(xcp, tt - 1)
                for pt in range(NPT):
                    bank = (tt % 2) * 2 + pt // 4
                    ins = t.transpose(PSb(bank, (pt % 4) * 128, 128),
                                      xstage[:, (tt % 2) * D + pt * 128:
                                             (tt % 2) * D + (pt + 1) * 128],
                                      idf[:])
                ins.then_inc(xtp, 1)

        @blk.vector
        def _(v):
            for tt in range(NTT):
                v.wait_ge(xtp, tt + 1)
                for pt in range(NPT):
                    bank = (tt % 2) * 2 + pt // 4
                    ins = v.tensor_copy(xT[:, pt * TP + tt * 128: pt * TP + (tt + 1) * 128],
                                        PSb(bank, (pt % 4) * 128, 128))
                ins.then_inc(xcp, 1)

        C.step("vector", lambda v: _last(v.wait_ge(xcp, NTT),
                                         v.tensor_copy(h["idf"][0:1, 0:1], h["idf"][0:1, 0:1])))
        C.flush()


def emit_stage_a(h):
    nc, p, blk, C, dram = h["nc"], h["p"], h["blk"], h["C"], h["dram"]
    PSb, PSf, sbuf = h["PSb"], h["PSf"], h["sbuf"]
    ones1f, onesrf, idf, idb, epsA, epsB = (h[k] for k in
                                            ("ones1f", "onesrf", "idf", "idb", "epsA", "epsB"))
    RTMP, RB0, RB1, m0, r0 = h["RTMP"], h["RB0"], h["RB1"], h["m0"], h["r0"]
    ln_stats_rows, bcast_row, normalize, mm_acc = (
        h[k] for k in ("ln_stats_rows", "bcast_row", "normalize", "mm_acc"))
    d_kT, d_vh, d_qT, d_oT, d_xres = (dram[k] for k in
                                      ("kT_scr", "v_scr", "qT_scr", "oT_scr", "xres"))
    Wt = lambda w, kt, mt: w[:, kt * D + mt * 128: kt * D + (mt + 1) * 128]

    vecs = sbuf("vecs_a", [128, 24], F32)
    for i, nm in enumerate(("b_proj", "ln_g", "ln_b")):
        C.step("sync", lambda sy, i=i, nm=nm: sy.dma_start(
            out=vecs[:, i * 8:(i + 1) * 8],
            in_=dram[nm].rearrange("(t p) -> p t", p=128),
            allow_slow_non_contiguous=True), dma=True)
    BPJ, LNG, LNB = vecs[:, 0:8], vecs[:, 8:16], vecs[:, 16:24]

    with contextlib.ExitStack() as s1:
        xT = s1.enter_context(nc.sbuf_tensor("xT", [128, NPT * TP], F32))
        XT = lambda pt, tc: xT[:, pt * TP + tc * TC: pt * TP + (tc + 1) * TC]
        emit_transpose_in(h, xT)

        with contextlib.ExitStack() as ph:
            sq = ph.enter_context(nc.sbuf_tensor("sq_a", [128, TC], F32))
            for tc in range(NTC):
                ln_stats_rows(C, lambda pt, tc=tc: XT(pt, tc), NPT, TC, sq,
                              m0[:, tc * TC:(tc + 1) * TC], r0[:, tc * TC:(tc + 1) * TC],
                              RTMP[:, 0:TC], epsA, D)
            C.flush()

        with contextlib.ExitStack() as ph:
            wk_sb = ph.enter_context(nc.sbuf_tensor("wk_sb", [128, NPT * D], BF16))
            wv_sb = ph.enter_context(nc.sbuf_tensor("wv_sb", [128, NPT * D], BF16))
            bstage = ph.enter_context(nc.sbuf_tensor("bstage", [128, 4 * D], F32))
            bT = ph.enter_context(nc.sbuf_tensor("bT", [128, NPT * BCH], F32))
            lnb = ph.enter_context(nc.sbuf_tensor("lnb", [128, NPT * BCH], BF16))
            kst = ph.enter_context(nc.sbuf_tensor("kst", [128, NPT * BCH], BF16))
            vst = ph.enter_context(nc.sbuf_tensor("vst", [128, NPT * BCH], F32))
            vks = ph.enter_context(nc.sbuf_tensor("vks", [128, 4 * 130], BF16))
            sqb = bstage

            for dst, nm in ((wk_sb, "w_k"), (wv_sb, "w_v")):
                C.step("gpsimd", lambda g, dst=dst, nm=nm: g.dma_start(
                    out=dst[:].rearrange("p (k n) -> p k n", n=D),
                    in_=dram[nm].rearrange("(k p) n -> p k n", p=128)), dma=True)
            C.step("vector", lambda v: v.memset(vks[:], 1.0))

            NT4 = BCH // 128
            ntiles = NT4 * NPT
            for bc in range(NBC):
                C.step("sync", lambda sy, bc=bc: sy.dma_start(
                    out=bstage[:, 0:NT4 * D].rearrange("p (t n) -> p t n", n=D),
                    in_=dram["xb"][bc * BCH:(bc + 1) * BCH, :].rearrange(
                        "(t p) n -> p t n", p=128)), dma=True)
                for g0 in range(0, ntiles, 16):
                    gn = min(16, ntiles - g0)
                    def t_tph(t, g0=g0, gn=gn):
                        for i in range(gn):
                            pt, tt4 = divmod(g0 + i, NT4)
                            ins = t.transpose(PSb(i // 4, (i % 4) * 128, 128),
                                              bstage[:, tt4 * D + pt * 128:
                                                     tt4 * D + (pt + 1) * 128],
                                              idf[:])
                        return ins
                    C.step("tensor", t_tph)
                    def v_cph(v, g0=g0, gn=gn):
                        for i in range(gn):
                            pt, tt4 = divmod(g0 + i, NT4)
                            ins = v.tensor_copy(
                                bT[:, pt * BCH + tt4 * 128: pt * BCH + (tt4 + 1) * 128],
                                PSb(i // 4, (i % 4) * 128, 128))
                        return ins
                    C.step("vector", v_cph)
                ln_stats_rows(C, lambda pt: bT[:, pt * BCH:(pt + 1) * BCH], NPT, BCH,
                              sqb, RB0, RB1, RTMP[:, 0:BCH], epsA, D)
                bcast_row(C, RB0, BCH, 6)
                bcast_row(C, RB1, BCH, 7)
                normalize(C, lambda pt: bT[:, pt * BCH:(pt + 1) * BCH],
                          lambda pt: lnb[:, pt * BCH:(pt + 1) * BCH], NPT, BCH,
                          PSb(6, 0, BCH), PSb(7, 0, BCH))
                for mt in range(NPT):
                    C.step("tensor", lambda t, mt=mt: mm_acc(
                        t, PSb(mt % 2, 0, BCH), lambda kt: Wt(wk_sb, kt, mt),
                        lambda kt: lnb[:, kt * BCH:(kt + 1) * BCH], NPT))
                    C.step("vector", lambda v, mt=mt: v.tensor_copy(
                        kst[:, mt * BCH:(mt + 1) * BCH], PSb(mt % 2, 0, BCH)))
                    C.step("tensor", lambda t, mt=mt: mm_acc(
                        t, PSb(2 + mt % 2, 0, BCH), lambda kt: Wt(wv_sb, kt, mt),
                        lambda kt: lnb[:, kt * BCH:(kt + 1) * BCH], NPT))
                    C.step("vector", lambda v, mt=mt: v.tensor_copy(
                        vst[:, mt * BCH:(mt + 1) * BCH], PSb(2 + mt % 2, 0, BCH)))
                for mt in range(NPT):
                    C.step("sync", lambda sy, bc=bc, mt=mt: sy.dma_start(
                        out=d_kT[mt * 128:(mt + 1) * 128, bc * BCH:(bc + 1) * BCH],
                        in_=kst[:, mt * BCH:(mt + 1) * BCH]), dma=True)
                for pt in range(NPT):
                    C.step("tensor", lambda t, pt=pt: _last(*[
                        t.transpose(PSb(4 + tt4 // 4, (tt4 % 4) * 128, 128),
                                    vst[:, pt * BCH + tt4 * 128: pt * BCH + (tt4 + 1) * 128],
                                    idf[:]) for tt4 in range(NT4)]))
                    def v_vcp(v, pt=pt):
                        for tt4 in range(NT4):
                            v.tensor_copy(vks[:, tt4 * 130 + 0: tt4 * 130 + 64],
                                          PSb(4 + tt4 // 4, (tt4 % 4) * 128, 64))
                            ins = v.tensor_copy(vks[:, tt4 * 130 + 65: tt4 * 130 + 129],
                                                PSb(4 + tt4 // 4, (tt4 % 4) * 128 + 64, 64))
                        return ins
                    C.step("vector", v_vcp)
                    C.step("sync", lambda sy, bc=bc, pt=pt: sy.dma_start(
                        out=d_vh[pt, bc * BCH:(bc + 1) * BCH, :].rearrange(
                            "(t p) c -> p t c", p=128),
                        in_=vks[:, 0:NT4 * 130].rearrange("p (t c) -> p t c", c=130)), dma=True)
                C.flush()

        with contextlib.ExitStack() as ph:
            wq_sb = ph.enter_context(nc.sbuf_tensor("wq_sb", [128, NPT * D], BF16))
            lnq = ph.enter_context(nc.sbuf_tensor("lnq", [128, NPT * TC], BF16))
            qst = ph.enter_context(nc.sbuf_tensor("qst", [128, TC], BF16))
            C.step("gpsimd", lambda g: g.dma_start(
                out=wq_sb[:].rearrange("p (k n) -> p k n", n=D),
                in_=dram["w_q"].rearrange("(k p) n -> p k n", p=128)), dma=True)
            for tc in range(NTC):
                bcast_row(C, m0[:, tc * TC:(tc + 1) * TC], TC, 6)
                bcast_row(C, r0[:, tc * TC:(tc + 1) * TC], TC, 7)
                normalize(C, lambda pt, tc=tc: XT(pt, tc),
                          lambda pt: lnq[:, pt * TC:(pt + 1) * TC], NPT, TC,
                          PSb(6, 0, TC), PSb(7, 0, TC))
                for mt in range(NPT):
                    C.step("tensor", lambda t, mt=mt: mm_acc(
                        t, PSb(mt % 2, 0, TC), lambda kt: Wt(wq_sb, kt, mt),
                        lambda kt: lnq[:, kt * TC:(kt + 1) * TC], NPT))
                    C.step("vector", lambda v, mt=mt: v.tensor_copy(
                        qst[:], PSb(mt % 2, 0, TC)))
                    C.step("sync", lambda sy, mt=mt, tc=tc: sy.dma_start(
                        out=d_qT[mt * 128:(mt + 1) * 128, tc * TC:(tc + 1) * TC],
                        in_=qst[:]), dma=True)
                C.flush()

        C.step("sync", lambda sy: sy.dma_start(out=d_xres, in_=xT[:]), dma=True)
        C.flush()

    emit_attention(h, d_kT, d_vh, d_qT, d_oT)

    with contextlib.ExitStack() as s3:
        xT = s3.enter_context(nc.sbuf_tensor("xT5", [128, NPT * TP], F32))
        ybuf = s3.enter_context(nc.sbuf_tensor("ybuf", [128, NPT * TP], F32))
        wp_sb = s3.enter_context(nc.sbuf_tensor("wp_sb", [128, NPT * D], BF16))
        och = s3.enter_context(nc.sbuf_tensor("och", [128, NPT * TC], BF16))
        sq5 = s3.enter_context(nc.sbuf_tensor("sq5", [128, TC], F32))
        XT = lambda pt, tc: xT[:, pt * TP + tc * TC: pt * TP + (tc + 1) * TC]
        YT = lambda pt, tc: ybuf[:, pt * TP + tc * TC: pt * TP + (tc + 1) * TC]
        m1, r1 = m0, r0

        C.step("sync", lambda sy: sy.dma_start(out=xT[:], in_=d_xres), dma=True)
        C.step("gpsimd", lambda g: g.dma_start(
            out=wp_sb[:].rearrange("p (k n) -> p k n", n=D),
            in_=dram["w_proj"].rearrange("(k p) n -> p k n", p=128)), dma=True)

        for tc in range(NTC):
            C.step("sync", lambda sy, tc=tc: sy.dma_start(
                out=och[:].rearrange("p (k n) -> p k n", n=TC),
                in_=d_oT[:, tc * TC:(tc + 1) * TC].rearrange("(k p) n -> p k n", p=128)), dma=True)
            bcast_row(C, m0[:, tc * TC:(tc + 1) * TC], TC, 6)
            bcast_row(C, r0[:, tc * TC:(tc + 1) * TC], TC, 7)
            for mt in range(NPT):
                C.step("tensor", lambda t, mt=mt, tc=tc: mm_acc(
                    t, PSb(mt % 2, 0, TC), lambda kt: Wt(wp_sb, kt, mt),
                    lambda kt: och[:, kt * TC:(kt + 1) * TC], NPT))
                def v_y(v, mt=mt, tc=tc):
                    y = YT(mt, tc)
                    i0 = v.tensor_scalar_add(y, PSb(mt % 2, 0, TC), BPJ[:, mt:mt + 1])
                    p.vg(v, i0)
                    i1 = v.tensor_tensor(sq5[:], XT(mt, tc), PSb(6, 0, TC), ALU.subtract)
                    p.vg(v, i1)
                    i2 = v.tensor_tensor(sq5[:], sq5[:], PSb(7, 0, TC), ALU.mult)
                    p.vg(v, i2)
                    return v.tensor_tensor(y, y, sq5[:], ALU.add)
                C.step("vector", v_y)
            C.flush()
        for tc in range(NTC):
            ln_stats_rows(C, lambda pt, tc=tc: YT(pt, tc), NPT, TC, sq5,
                          m1[:, tc * TC:(tc + 1) * TC], r1[:, tc * TC:(tc + 1) * TC],
                          RTMP[:, 0:TC], epsB, D)
        C.flush()
        for tc in range(NTC):
            bcast_row(C, m1[:, tc * TC:(tc + 1) * TC], TC, 6)
            bcast_row(C, r1[:, tc * TC:(tc + 1) * TC], TC, 7)
            def v_fin(v, tc=tc):
                for pt in range(NPT):
                    n = sq5[:]
                    i0 = v.tensor_tensor(n, YT(pt, tc), PSb(6, 0, TC), ALU.subtract)
                    p.vg(v, i0)
                    i1 = v.tensor_tensor(n, n, PSb(7, 0, TC), ALU.mult)
                    p.vg(v, i1)
                    i2 = v.scalar_tensor_tensor(XT(pt, tc), n, LNG[:, pt:pt + 1], XT(pt, tc),
                                                ALU.mult, ALU.add)
                    p.vg(v, i2)
                    i3 = v.tensor_scalar_add(XT(pt, tc), XT(pt, tc), LNB[:, pt:pt + 1])
                return i3
            C.step("vector", v_fin)
            C.flush()
        C.step("sync", lambda sy: sy.dma_start(out=d_xres, in_=xT[:]), dma=True)
        C.flush()


def emit_stage_b(h):
    nc, p, blk, C, dram = h["nc"], h["p"], h["blk"], h["C"], h["dram"]
    PSb, PSf, sbuf = h["PSb"], h["PSf"], h["sbuf"]
    ones1f, onesrf, idf, epsA = h["ones1f"], h["onesrf"], h["idf"], h["epsA"]
    RTMP, m0, r0 = h["RTMP"], h["m0"], h["r0"]
    ln_stats_rows, bcast_row, normalize = (h[k] for k in
                                           ("ln_stats_rows", "bcast_row", "normalize"))
    d_xres, d_ln1 = dram["xres"], dram["ln1_scr"]
    NKI = I_FF // 128
    w8T = sbuf("w8T", [8, TP], F32)

    with contextlib.ExitStack() as s1:
        xT = s1.enter_context(nc.sbuf_tensor("xTb", [128, NPT * TP], F32))
        XT = lambda pt, tc: xT[:, pt * TP + tc * TC: pt * TP + (tc + 1) * TC]
        sq = s1.enter_context(nc.sbuf_tensor("sq_b", [128, TC], F32))
        lnc = s1.enter_context(nc.sbuf_tensor("lnc_b", [128, NPT * TC], BF16))
        rtf = s1.enter_context(nc.sbuf_tensor("rtf_b", [128, NPT * NE], F32))
        lnf = s1.enter_context(nc.sbuf_tensor("lnf_b", [128, NPT * TC], F32))
        pr = s1.enter_context(nc.sbuf_tensor("pr_b", [128, NTT * NE], F32))
        prb = s1.enter_context(nc.sbuf_tensor("prb_b", [128, NTT * NE], F32))
        w8 = s1.enter_context(nc.sbuf_tensor("w8_b", [128, NTT * NE], F32))
        neg = s1.enter_context(nc.sbuf_tensor("neg_b", [128, NE], F32))
        rb1 = s1.enter_context(nc.sbuf_tensor("rb1_b", [1, NE], F32))
        rbb = s1.enter_context(nc.sbuf_tensor("rbb_b", [128, NE], F32))
        rowm = s1.enter_context(nc.sbuf_tensor("rowm_b", [128, 4], F32))
        mskd = s1.enter_context(nc.sbuf_tensor("mskd_b", [128, NE], F32))
        gei = s1.enter_context(nc.sbuf_tensor("gei_b", [128, NE], mybir.dt.int32))

        C.step("sync", lambda sy: sy.dma_start(out=xT[:], in_=d_xres), dma=True)
        for tc in range(NTC):
            ln_stats_rows(C, lambda pt, tc=tc: XT(pt, tc), NPT, TC, sq,
                          m0[:, tc * TC:(tc + 1) * TC], r0[:, tc * TC:(tc + 1) * TC],
                          RTMP[:, 0:TC], epsA, D)
        C.flush()

        C.step("sync", lambda sy: sy.dma_start(out=rb1[:], in_=dram["r_bias"][None, :]),
               dma=True)
        C.step("sync", lambda sy: sy.dma_start(
            out=rtf[:].rearrange("p (k n) -> p k n", n=NE),
            in_=dram["router_T"].rearrange("(k p) n -> p k n", p=128)), dma=True)
        bcast_row(C, rb1[0:1, :], NE, 5)
        C.step("vector", lambda v: _last(v.memset(neg[:], -1e30),
                                         v.tensor_copy(rbb[:], PSb(5, 0, NE))))
        for tc in range(NTC):
            bcast_row(C, m0[:, tc * TC:(tc + 1) * TC], TC, 6)
            bcast_row(C, r0[:, tc * TC:(tc + 1) * TC], TC, 7)
            normalize(C, lambda pt, tc=tc: XT(pt, tc),
                      lambda pt: lnf[:, pt * TC:(pt + 1) * TC], NPT, TC,
                      PSb(6, 0, TC), PSb(7, 0, TC))
            for st3 in range(TC // 128):
                tt = tc * (TC // 128) + st3
                C.step("tensor", lambda t, st3=st3: _last(*[
                    t.matmul(PSb(4, 0, NE)[0:128, :],
                             lnf[:, kt * TC + st3 * 128: kt * TC + (st3 + 1) * 128],
                             rtf[:, kt * NE:(kt + 1) * NE],
                             start=(kt == 0), stop=(kt == NPT - 1)) for kt in range(NPT)]))
                def v_lg(v, tt=tt):
                    return v.tensor_tensor(pr[:, tt * NE:(tt + 1) * NE],
                                           PSb(4, 0, NE)[0:128, :], rbb[:], ALU.add)
                C.step("vector", v_lg)
                C.step("scalar", lambda s, tt=tt: s.activation(
                    prb[:, tt * NE:(tt + 1) * NE], pr[:, tt * NE:(tt + 1) * NE],
                    AF.Sigmoid))
                def v_top2(v, tt=tt):
                    lz = pr[:, tt * NE:(tt + 1) * NE]
                    pz = prb[:, tt * NE:(tt + 1) * NE]
                    wz = w8[:, tt * NE:(tt + 1) * NE]
                    i1 = v.reduce_max(rowm[:, 0:1], lz, axis=mybir.AxisListType.X)
                    p.vg(v, i1)
                    i2 = v.tensor_scalar(gei[:], lz, rowm[:, 0:1], None, ALU.is_ge)
                    p.vg(v, i2)
                    i3 = v.select(mskd[:], gei[:], neg[:], lz, add_drain=True)
                    p.vg(v, i3)
                    return v.reduce_max(rowm[:, 1:2], mskd[:], axis=mybir.AxisListType.X)
                C.step("vector", v_top2)
                C.step("scalar", lambda s: s.activation(rowm[:, 2:4], rowm[:, 0:2],
                                                        AF.Sigmoid))
                def v_top2b(v, tt=tt):
                    lz = pr[:, tt * NE:(tt + 1) * NE]
                    pz = prb[:, tt * NE:(tt + 1) * NE]
                    wz = w8[:, tt * NE:(tt + 1) * NE]
                    i5 = v.tensor_tensor(rowm[:, 2:3], rowm[:, 2:3], rowm[:, 3:4], ALU.add)
                    p.vg(v, i5)
                    i6 = v.reciprocal(rowm[:, 3:4], rowm[:, 2:3])
                    p.vg(v, i6)
                    i7 = v.tensor_scalar(wz, lz, rowm[:, 1:2], None, ALU.is_ge)
                    p.vg(v, i7)
                    i8 = v.tensor_tensor(wz, wz, pz, ALU.mult)
                    p.vg(v, i8)
                    return v.tensor_scalar(wz, wz, rowm[:, 3:4], None, ALU.mult)
                C.step("vector", v_top2b)
                C.step("tensor", lambda t, tt=tt: t.transpose(
                    PSb(6, 0, 128)[0:NE, :], w8[:, tt * NE:(tt + 1) * NE], idf[:]))
                C.step("vector", lambda v, tt=tt: v.tensor_copy(
                    w8T[:, tt * 128:(tt + 1) * 128], PSb(6, 0, 128)[0:NE, :]))
            def v_cast(v, tc=tc):
                for pt in range(NPT):
                    i1 = v.tensor_copy(lnc[:, pt * TC:(pt + 1) * TC],
                                       lnf[:, pt * TC:(pt + 1) * TC])
                return i1
            C.step("vector", v_cast)
            for pt in range(NPT):
                C.step("sync", lambda sy, pt=pt, tc=tc: sy.dma_start(
                    out=d_ln1[pt * 128:(pt + 1) * 128, tc * TC:(tc + 1) * TC],
                    in_=lnc[:, pt * TC:(pt + 1) * TC]), dma=True)
            C.flush()

    with contextlib.ExitStack() as ph:
        wg = ph.enter_context(nc.sbuf_tensor("wg_b", [128, NPT * I_FF], BF16))
        wu = ph.enter_context(nc.sbuf_tensor("wu_b", [128, NPT * I_FF], BF16))
        wd = ph.enter_context(nc.sbuf_tensor("wd_b", [128, NKI * D], BF16))
        lntc = ph.enter_context(nc.sbuf_tensor("lntc_b", [128, 2 * NPT * TC], BF16))
        gu = ph.enter_context(nc.sbuf_tensor("gu_b", [128, NKI * TC], BF16))
        wbb = ph.enter_context(nc.sbuf_tensor("wbb_b", [128, TC], F32))
        wrow = ph.enter_context(nc.sbuf_tensor("wrow_b", [1, TC], F32))
        tmp5 = ph.enter_context(nc.sbuf_tensor("tmp5_b", [128, 2 * TC], F32))
        WgT = lambda kt, mt: wg[:, kt * I_FF + mt * 128: kt * I_FF + (mt + 1) * 128]
        WuT = lambda kt, mt: wu[:, kt * I_FF + mt * 128: kt * I_FF + (mt + 1) * 128]
        WdT = lambda kt, mt: wd[:, kt * D + mt * 128: kt * D + (mt + 1) * 128]
        LNS = lambda pair: lntc[:, (pair % 2) * NPT * TC:(pair % 2 + 1) * NPT * TC]
        TMP = lambda i8: tmp5[:, (i8 % 2) * TC:(i8 % 2 + 1) * TC]

        sGt, sSl, sUp, sGv, sDn, sEv, sBc, sWc = (p.sem(x) for x in
            ("b2Gt", "b2Sl", "b2Up", "b2Gv", "b2Dn", "b2Ev", "b2Bc", "b2Wc"))
        sWg, sWu, sWd = p.sem("b2Wg"), p.sem("b2Wu"), p.sem("b2Wd")
        sLn = [p.sem("b2Ln0"), p.sem("b2Ln1")]
        sWr = p.sem("b2Wr")
        sAc = [p.sem("b2Ac0"), p.sem("b2Ac1")]
        C.flush()
        tl = C.tail()
        NP2 = NTC * 8
        srcs_of = lambda pe: ((dram["sh_gate_T"], dram["sh_up_T"], dram["sh_down_T"])
                              if pe == 0 else
                              (dram["gate_w"][pe - 1], dram["up_w"][pe - 1],
                               dram["down_w"][pe - 1]))

        @blk.gpsimd
        def _(g):
            if tl:
                g.wait_ge(p.sem(tl[0]), tl[1])
            for pe in range(NE + 1):
                srcs = srcs_of(pe)
                if pe >= 1:
                    g.wait_ge(sGt, pe * NTC * 16)
                g.dma_start(out=wg[:].rearrange("p (k n) -> p k n", n=I_FF),
                            in_=srcs[0].rearrange("(k p) n -> p k n", p=128)
                            ).then_inc(sWg, 16)
                if pe >= 1:
                    g.wait_ge(sUp, pe * NTC * 16)
                g.dma_start(out=wu[:].rearrange("p (k n) -> p k n", n=I_FF),
                            in_=srcs[1].rearrange("(k p) n -> p k n", p=128)
                            ).then_inc(sWu, 16)
                if pe >= 1:
                    g.wait_ge(sDn, pe * NP2)
                    g.wait_ge(sAc[0], (pe * NP2 // 2) * 16)
                    g.wait_ge(sAc[1], (pe * NP2 // 2) * 16)
                g.dma_start(out=wd[:].rearrange("p (k n) -> p k n", n=D),
                            in_=srcs[2].rearrange("(k p) n -> p k n", p=128)
                            ).then_inc(sWd, 16)
                for tc in range(NTC):
                    for mtD in range(NPT):
                        i8 = (pe * NTC + tc) * 8 + mtD
                        g.wait_ge(sEv, i8 + 1)
                        g.dma_start(
                            out=d_xres[:, mtD * TP + tc * TC: mtD * TP + (tc + 1) * TC],
                            in_=TMP(i8), accum_op=ALU.add).then_inc(sAc[i8 % 2], 16)

        @blk.sync
        def _(sy):
            if tl:
                sy.wait_ge(p.sem(tl[0]), tl[1])
            for pe in range(NE + 1):
                for tc in range(NTC):
                    pair = pe * NTC + tc
                    if pair >= 2:
                        sy.wait_ge(sUp, (pair - 1) * 16)
                    sy.dma_start(out=LNS(pair).rearrange("p (k n) -> p k n", n=TC),
                                 in_=d_ln1[:, tc * TC:(tc + 1) * TC].rearrange(
                                     "(k p) n -> p k n", p=128)
                                 ).then_inc(sLn[pair % 2], 16)
                    if pe >= 1:
                        wb = (pe - 1) * NTC + tc
                        sy.wait_ge(sBc, wb)
                        sy.dma_start(out=wrow[:],
                                     in_=w8T[pe - 1:pe, tc * TC:(tc + 1) * TC]
                                     ).then_inc(sWr, 16)

        @blk.tensor
        def _(t):
            if tl:
                t.wait_ge(p.sem(tl[0]), tl[1])
            for pe in range(NE + 1):
                for tc in range(NTC):
                    pair = pe * NTC + tc
                    t.wait_ge(sLn[pair % 2], (pair // 2 + 1) * 16)
                    if pe >= 1:
                        wb = (pe - 1) * NTC + tc
                        t.wait_ge(sWr, (wb + 1) * 16)
                        if wb >= 1:
                            t.wait_ge(sWc, wb)
                        t.matmul(PSb(6, 0, TC), onesrf[:], wrow[0:1, :],
                                 start=True, stop=True,
                                 skip_group_check=True).then_inc(sBc, 1)
                    if tc == 0:
                        t.wait_ge(sWg, (pe + 1) * 16)
                        t.wait_ge(sWu, (pe + 1) * 16)
                    for mt in range(NKI):
                        i16 = pair * 16 + mt
                        if i16 >= 2:
                            t.wait_ge(sSl, i16 - 1)
                        for kt in range(NPT):
                            ins = t.matmul(PSb(mt % 2, 0, TC), WgT(kt, mt),
                                           LNS(pair)[:, kt * TC:(kt + 1) * TC],
                                           start=(kt == 0), stop=(kt == NPT - 1),
                                           skip_group_check=True)
                        ins.then_inc(sGt, 1)
                        if i16 >= 2:
                            t.wait_ge(sGv, i16 - 1)
                        for kt in range(NPT):
                            ins = t.matmul(PSb(2 + mt % 2, 0, TC), WuT(kt, mt),
                                           LNS(pair)[:, kt * TC:(kt + 1) * TC],
                                           start=(kt == 0), stop=(kt == NPT - 1),
                                           skip_group_check=True)
                        ins.then_inc(sUp, 1)
                    if tc == 0:
                        t.wait_ge(sWd, (pe + 1) * 16)
                    t.wait_ge(sGv, pair * 16 + 16)
                    for mtD in range(NPT):
                        i8 = pair * 8 + mtD
                        if i8 >= 2:
                            t.wait_ge(sEv, i8 - 1)
                        for kt in range(NKI):
                            ins = t.matmul(PSb(4 + mtD % 2, 0, TC), WdT(kt, mtD),
                                           gu[:, kt * TC:(kt + 1) * TC],
                                           start=(kt == 0), stop=(kt == NKI - 1),
                                           skip_group_check=True)
                        ins.then_inc(sDn, 1)

        @blk.scalar
        def _(s):
            for pe in range(NE + 1):
                for tc in range(NTC):
                    pair = pe * NTC + tc
                    for mt in range(NKI):
                        i16 = pair * 16 + mt
                        s.wait_ge(sGt, i16 + 1)
                        if pair >= 1:
                            s.wait_ge(sDn, pair * 8)
                        s.activation(gu[:, mt * TC:(mt + 1) * TC],
                                     PSb(mt % 2, 0, TC), AF.Silu).then_inc(sSl, 1)

        @blk.vector
        def _(v):
            for pe in range(NE + 1):
                for tc in range(NTC):
                    pair = pe * NTC + tc
                    if pe >= 1:
                        wb = (pe - 1) * NTC + tc
                        v.wait_ge(sBc, wb + 1)
                        v.wait_ge(sEv, pair * 8)
                        v.tensor_copy(wbb[:], PSb(6, 0, TC)).then_inc(sWc, 1)
                    for mt in range(NKI):
                        i16 = pair * 16 + mt
                        v.wait_ge(sUp, i16 + 1)
                        v.wait_ge(sSl, i16 + 1)
                        v.tensor_tensor(gu[:, mt * TC:(mt + 1) * TC],
                                        gu[:, mt * TC:(mt + 1) * TC],
                                        PSb(2 + mt % 2, 0, TC), ALU.mult).then_inc(sGv, 1)
                    for mtD in range(NPT):
                        i8 = pair * 8 + mtD
                        v.wait_ge(sDn, i8 + 1)
                        if i8 >= 2:
                            v.wait_ge(sAc[i8 % 2], ((i8 - 2) // 2 + 1) * 16)
                        if pe == 0:
                            v.tensor_copy(TMP(i8), PSb(4 + mtD % 2, 0, TC)
                                          ).then_inc(sEv, 1)
                        else:
                            v.tensor_tensor(TMP(i8), PSb(4 + mtD % 2, 0, TC), wbb[:],
                                            ALU.mult).then_inc(sEv, 1)

        def joinf(v):
            v.wait_ge(sAc[0], ((NE + 1) * NP2 // 2) * 16)
            v.wait_ge(sAc[1], ((NE + 1) * NP2 // 2) * 16)
            return v.tensor_copy(wbb[0:1, 0:1], wbb[0:1, 0:1])
        C.step("vector", joinf)
        C.flush()


def _emit_shift_mix(h, C, ln_sb, mix_pp_list, out_list, dtmp, tc):
    p = h["p"]
    def v_mix(v):
        for pt in range(NPT):
            base = pt * (TP + 1)
            cur = ln_sb[:, base + 1 + tc * TC: base + 1 + (tc + 1) * TC]
            prv = ln_sb[:, base + tc * TC: base + (tc + 1) * TC]
            i0 = v.tensor_tensor(dtmp[:], cur, prv, ALU.subtract)
            p.vg(v, i0)
            for mix_pp, out in zip(mix_pp_list, out_list):
                i1 = v.scalar_tensor_tensor(
                    out[:, pt * TC:(pt + 1) * TC], dtmp[:], mix_pp[:, pt:pt + 1], prv,
                    ALU.mult, ALU.add)
            if pt < NPT - 1:
                p.vg(v, i1)
        return i1
    C.step("vector", v_mix)


def _emit_ln_masked(h, C, d_xres, ln_sb, name):
    nc, p, blk = h["nc"], h["p"], h["blk"]
    PSb = h["PSb"]
    RTMP, m0, r0, epsA = h["RTMP"], h["m0"], h["r0"], h["epsA"]
    ln_stats_rows, bcast_row = h["ln_stats_rows"], h["bcast_row"]
    maskb = h["maskb"]
    with contextlib.ExitStack() as s0:
        xT = s0.enter_context(nc.sbuf_tensor("xT_" + name, [128, NPT * TP], F32))
        XT = lambda pt, tc: xT[:, pt * TP + tc * TC: pt * TP + (tc + 1) * TC]
        sq = s0.enter_context(nc.sbuf_tensor("sq_" + name, [128, TC], F32))
        C.step("sync", lambda sy: sy.dma_start(out=xT[:], in_=d_xres), dma=True)
        C.step("vector", lambda v: _last(*[
            v.memset(ln_sb[:, pt * (TP + 1): pt * (TP + 1) + 1], 0.0)
            for pt in range(NPT)]))
        for tc in range(NTC):
            ln_stats_rows(C, lambda pt, tc=tc: XT(pt, tc), NPT, TC, sq,
                          m0[:, tc * TC:(tc + 1) * TC], r0[:, tc * TC:(tc + 1) * TC],
                          RTMP[:, 0:TC], epsA, D)
            bcast_row(C, m0[:, tc * TC:(tc + 1) * TC], TC, 6)
            bcast_row(C, r0[:, tc * TC:(tc + 1) * TC], TC, 7)
            def v_lnm(v, tc=tc):
                for pt in range(NPT):
                    dst = ln_sb[:, pt * (TP + 1) + 1 + tc * TC: pt * (TP + 1) + 1 + (tc + 1) * TC]
                    i0 = v.tensor_tensor(sq[:], XT(pt, tc), PSb(6, 0, TC), ALU.subtract)
                    p.vg(v, i0)
                    i1 = v.tensor_tensor(sq[:], sq[:], PSb(7, 0, TC), ALU.mult)
                    p.vg(v, i1)
                    i2 = v.tensor_tensor(dst, sq[:], maskb[:, tc * TC:(tc + 1) * TC],
                                         ALU.mult)
                    if pt < NPT - 1:
                        p.vg(v, i2)
                return i2
            C.step("vector", v_lnm)
            C.flush()


def emit_stage_c(h):
    nc, p, blk, C, dram = h["nc"], h["p"], h["blk"], h["C"], h["dram"]
    PSb, sbuf = h["PSb"], h["sbuf"]
    d_xres = dram["xres"]
    d_kf, d_vf, d_sr, d_rw = (dram[k] for k in ("kf_scr", "vf_scr", "sr_scr", "rw_scr"))
    Wt = lambda w, kt, mt: w[:, kt * D + mt * 128: kt * D + (mt + 1) * 128]

    if "maskb" not in h:
        h["maskb"] = sbuf("maskb", [128, TP], F32)
        C.step("sync", lambda sy: sy.dma_start(out=h["maskb"][:], in_=dram["mask"]),
               dma=True)
    maskb = h["maskb"]
    vecsC = sbuf("vecsC", [128, 48], F32)
    for i, nm in enumerate(("l1_w", "l1_u", "l1_mk", "l1_mv", "l1_mr")):
        C.step("sync", lambda sy, i=i, nm=nm: sy.dma_start(
            out=vecsC[:, i * 8:(i + 1) * 8],
            in_=dram[nm].rearrange("(t p) -> p t", p=128),
            allow_slow_non_contiguous=True), dma=True)
    WV, UV, MK, MV, MR = (vecsC[:, i * 8:(i + 1) * 8] for i in range(5))
    DCOL = vecsC[:, 40:48]
    EU = UV

    ln2 = sbuf("ln2_sb", [128, NPT * (TP + 1)], BF16)
    _emit_ln_masked(h, C, d_xres, ln2, "c")

    def s_wexp(s):
        s.activation(DCOL[:, :], WV[:, :], AF.Exp)
        return s.activation(vecsC[:, 8:16], UV[:, :], AF.Exp)
    C.step("scalar", s_wexp)
    C.step("vector", lambda v: v.tensor_scalar_mul(DCOL[:, :], DCOL[:, :], -1.0))
    C.step("scalar", lambda s: s.activation(DCOL[:, :], DCOL[:, :], AF.Exp))

    with contextlib.ExitStack() as s1:
        wk = s1.enter_context(nc.sbuf_tensor("wkC", [128, NPT * D], BF16))
        wv = s1.enter_context(nc.sbuf_tensor("wvC", [128, NPT * D], BF16))
        wr = s1.enter_context(nc.sbuf_tensor("wrC", [128, NPT * D], BF16))
        xk = s1.enter_context(nc.sbuf_tensor("xkC", [128, NPT * TC], BF16))
        xv = s1.enter_context(nc.sbuf_tensor("xvC", [128, NPT * TC], BF16))
        xr = s1.enter_context(nc.sbuf_tensor("xrC", [128, NPT * TC], BF16))
        dtmp = s1.enter_context(nc.sbuf_tensor("dtmpC", [128, TC], F32))
        kst = s1.enter_context(nc.sbuf_tensor("kstC", [128, TC], F32))
        srt = s1.enter_context(nc.sbuf_tensor("srtC", [128, TC], BF16))
        for dst, nm in ((wk, "Wk_T"), (wv, "Wv_T"), (wr, "Wr_T")):
            C.step("gpsimd", lambda g, dst=dst, nm=nm: g.dma_start(
                out=dst[:].rearrange("p (k n) -> p k n", n=D),
                in_=dram[nm].rearrange("(k p) n -> p k n", p=128)), dma=True)
        for tc in range(NTC):
            _emit_shift_mix(h, C, ln2, [MK, MV, MR], [xk, xv, xr], dtmp, tc)
            for mt in range(NPT):
                C.step("tensor", lambda t, mt=mt: _last(*[
                    t.matmul(PSb(mt % 2, 0, TC), Wt(wk, kt, mt),
                             xk[:, kt * TC:(kt + 1) * TC],
                             start=(kt == 0), stop=(kt == NPT - 1)) for kt in range(NPT)]))
                C.step("vector", lambda v, mt=mt: v.tensor_copy(kst[:], PSb(mt % 2, 0, TC)))
                C.step("sync", lambda sy, mt=mt, tc=tc: sy.dma_start(
                    out=d_kf[mt * 128:(mt + 1) * 128, tc * TC:(tc + 1) * TC],
                    in_=kst[:]), dma=True)
                C.step("tensor", lambda t, mt=mt: _last(*[
                    t.matmul(PSb(2 + mt % 2, 0, TC), Wt(wv, kt, mt),
                             xv[:, kt * TC:(kt + 1) * TC],
                             start=(kt == 0), stop=(kt == NPT - 1)) for kt in range(NPT)]))
                C.step("vector", lambda v, mt=mt: v.tensor_copy(kst[:], PSb(2 + mt % 2, 0, TC)))
                C.step("sync", lambda sy, mt=mt, tc=tc: sy.dma_start(
                    out=d_vf[mt * 128:(mt + 1) * 128, tc * TC:(tc + 1) * TC],
                    in_=kst[:]), dma=True)
                C.step("tensor", lambda t, mt=mt: _last(*[
                    t.matmul(PSb(4 + mt % 2, 0, TC), Wt(wr, kt, mt),
                             xr[:, kt * TC:(kt + 1) * TC],
                             start=(kt == 0), stop=(kt == NPT - 1)) for kt in range(NPT)]))
                C.step("scalar", lambda s, mt=mt: s.activation(
                    srt[:], PSb(4 + mt % 2, 0, TC), AF.Sigmoid))
                C.step("sync", lambda sy, mt=mt, tc=tc: sy.dma_start(
                    out=d_sr[mt * 128:(mt + 1) * 128, tc * TC:(tc + 1) * TC],
                    in_=srt[:]), dma=True)
            C.flush()

    with contextlib.ExitStack() as s2:
        kf = s2.enter_context(nc.sbuf_tensor("kfC", [128, TP], F32))
        vf = s2.enter_context(nc.sbuf_tensor("vfC", [128, TP], F32))
        af = s2.enter_context(nc.sbuf_tensor("afC", [128, TP], F32))
        avf = s2.enter_context(nc.sbuf_tensor("avC", [128, TP], F32))
        Pf = s2.enter_context(nc.sbuf_tensor("PfC", [128, TP + 1], F32))
        Qf = s2.enter_context(nc.sbuf_tensor("QfC", [128, TP + 1], F32))
        dtl = s2.enter_context(nc.sbuf_tensor("dtlC", [128, TP], F32))
        srf = s2.enter_context(nc.sbuf_tensor("srfC", [128, TP], BF16))
        rwt = s2.enter_context(nc.sbuf_tensor("rwtC", [128, TP], BF16))
        for pt in range(NPT):
            C.step("sync", lambda sy, pt=pt: sy.dma_start(
                out=kf[:], in_=d_kf[pt * 128:(pt + 1) * 128, :]), dma=True)
            C.step("sync", lambda sy, pt=pt: sy.dma_start(
                out=vf[:], in_=d_vf[pt * 128:(pt + 1) * 128, :]), dma=True)
            C.step("sync", lambda sy, pt=pt: sy.dma_start(
                out=srf[:], in_=d_sr[pt * 128:(pt + 1) * 128, :]), dma=True)
            def v_pre(v, pt=pt):
                i0 = v.memset(dtl[:], 1.0)
                p.vg(v, i0)
                i1 = v.tensor_scalar_mul(dtl[:], dtl[:], DCOL[:, pt:pt + 1])
                i2 = v.memset(Pf[:, 0:1], 0.0)
                return v.memset(Qf[:, 0:1], 0.0)
            C.step("vector", v_pre)
            C.step("scalar", lambda s: s.activation(af[:], kf[:], AF.Exp))
            def v_scan(v, pt=pt):
                i0 = v.tensor_tensor(af[:], af[:], maskb[:], ALU.mult)
                p.vg(v, i0)
                i1 = v.tensor_tensor(avf[:], af[:], vf[:], ALU.mult)
                p.vg(v, i1)
                i2 = v.tensor_tensor_scan(Pf[:, 1:TP + 1], dtl[:], avf[:], 0.0,
                                          ALU.mult, ALU.add)
                p.vg(v, i2)
                i3 = v.tensor_tensor_scan(Qf[:, 1:TP + 1], dtl[:], af[:], 0.0,
                                          ALU.mult, ALU.add)
                p.vg(v, i3)
                i4 = v.scalar_tensor_tensor(avf[:], avf[:], vecsC[:, 8 + pt:9 + pt],
                                            Pf[:, 0:TP], ALU.mult, ALU.add)
                p.vg(v, i4)
                i5 = v.scalar_tensor_tensor(af[:], af[:], vecsC[:, 8 + pt:9 + pt],
                                            Qf[:, 0:TP], ALU.mult, ALU.add)
                p.vg(v, i5)
                i6 = v.tensor_scalar_max(af[:], af[:], 1e-30)
                p.vg(v, i6)
                i7 = v.reciprocal(kf[:], af[:])
                p.vg(v, i7)
                i8 = v.tensor_tensor(avf[:], avf[:], kf[:], ALU.mult)
                p.vg(v, i8)
                return v.tensor_tensor(rwt[:], avf[:], srf[:], ALU.mult)
            C.step("vector", v_scan)
            C.step("sync", lambda sy, pt=pt: sy.dma_start(
                out=d_rw[pt * 128:(pt + 1) * 128, :], in_=rwt[:]), dma=True)
            C.flush()

    with contextlib.ExitStack() as s3:
        wo = s3.enter_context(nc.sbuf_tensor("woC", [128, NPT * D], BF16))
        rwc = s3.enter_context(nc.sbuf_tensor("rwcC", [128, NPT * TC], BF16))
        tmp6 = s3.enter_context(nc.sbuf_tensor("tmp6C", [128, TC], F32))
        C.step("gpsimd", lambda g: g.dma_start(
            out=wo[:].rearrange("p (k n) -> p k n", n=D),
            in_=dram["Wo_T"].rearrange("(k p) n -> p k n", p=128)), dma=True)
        for tc in range(NTC):
            C.step("sync", lambda sy, tc=tc: sy.dma_start(
                out=rwc[:].rearrange("p (k n) -> p k n", n=TC),
                in_=d_rw[:, tc * TC:(tc + 1) * TC].rearrange("(k p) n -> p k n", p=128)),
                dma=True)
            for mt in range(NPT):
                C.step("tensor", lambda t, mt=mt: _last(*[
                    t.matmul(PSb(mt % 2, 0, TC), Wt(wo, kt, mt),
                             rwc[:, kt * TC:(kt + 1) * TC],
                             start=(kt == 0), stop=(kt == NPT - 1)) for kt in range(NPT)]))
                C.step("vector", lambda v, mt=mt: v.tensor_copy(tmp6[:], PSb(mt % 2, 0, TC)))
                C.step("gpsimd", lambda g, mt=mt, tc=tc: g.dma_start(
                    out=d_xres[:, mt * TP + tc * TC: mt * TP + (tc + 1) * TC],
                    in_=tmp6[:], accum_op=ALU.add), dma=True)
            C.flush()


def emit_stage_d(h):
    nc, p, blk, C, dram = h["nc"], h["p"], h["blk"], h["C"], h["dram"]
    PSb, sbuf = h["PSb"], h["sbuf"]
    d_xres = dram["xres"]
    Wt = lambda w, kt, mt: w[:, kt * D + mt * 128: kt * D + (mt + 1) * 128]

    if "maskb" not in h:
        h["maskb"] = sbuf("maskb", [128, TP], F32)
        C.step("sync", lambda sy: sy.dma_start(out=h["maskb"][:], in_=dram["mask"]),
               dma=True)
    vecsD = sbuf("vecsD", [128, 16], F32)
    for i, nm in enumerate(("f_mur", "f_muk")):
        C.step("sync", lambda sy, i=i, nm=nm: sy.dma_start(
            out=vecsD[:, i * 8:(i + 1) * 8],
            in_=dram[nm].rearrange("(t p) -> p t", p=128),
            allow_slow_non_contiguous=True), dma=True)
    MUR, MUK = vecsD[:, 0:8], vecsD[:, 8:16]

    ln3 = sbuf("ln3_sb", [128, NPT * (TP + 1)], BF16)
    _emit_ln_masked(h, C, d_xres, ln3, "d")

    with contextlib.ExitStack() as s1:
        wrD = s1.enter_context(nc.sbuf_tensor("wrD", [128, NPT * D], BF16))
        wkD = s1.enter_context(nc.sbuf_tensor("wkD", [128, NPT * D], BF16))
        wvD = s1.enter_context(nc.sbuf_tensor("wvD", [128, NPT * D], BF16))
        xrD = s1.enter_context(nc.sbuf_tensor("xrD", [128, NPT * TC], BF16))
        xkD = s1.enter_context(nc.sbuf_tensor("xkD", [128, NPT * TC], BF16))
        dtmp = s1.enter_context(nc.sbuf_tensor("dtmpD", [128, TC], F32))
        srD = s1.enter_context(nc.sbuf_tensor("srD", [128, NPT * TC], BF16))
        kaD = s1.enter_context(nc.sbuf_tensor("kaD", [128, NPT * TC], BF16))
        tmpD = s1.enter_context(nc.sbuf_tensor("tmpD", [128, TC], F32))
        for dst, nm in ((wrD, "f_Wr_T"), (wkD, "f_Wk_T"), (wvD, "f_Wv_T")):
            C.step("gpsimd", lambda g, dst=dst, nm=nm: g.dma_start(
                out=dst[:].rearrange("p (k n) -> p k n", n=D),
                in_=dram[nm].rearrange("(k p) n -> p k n", p=128)), dma=True)
        for tc in range(NTC):
            _emit_shift_mix(h, C, ln3, [MUR, MUK], [xrD, xkD], dtmp, tc)
            for mt in range(NPT):
                C.step("tensor", lambda t, mt=mt: _last(*[
                    t.matmul(PSb(mt % 2, 0, TC), Wt(wrD, kt, mt),
                             xrD[:, kt * TC:(kt + 1) * TC],
                             start=(kt == 0), stop=(kt == NPT - 1)) for kt in range(NPT)]))
                C.step("scalar", lambda s, mt=mt: s.activation(
                    srD[:, mt * TC:(mt + 1) * TC], PSb(mt % 2, 0, TC), AF.Sigmoid))
                C.step("tensor", lambda t, mt=mt: _last(*[
                    t.matmul(PSb(2 + mt % 2, 0, TC), Wt(wkD, kt, mt),
                             xkD[:, kt * TC:(kt + 1) * TC],
                             start=(kt == 0), stop=(kt == NPT - 1)) for kt in range(NPT)]))
                def v_ka(v, mt=mt):
                    i0 = v.tensor_scalar_max(tmpD[:], PSb(2 + mt % 2, 0, TC), 0.0)
                    p.vg(v, i0)
                    return v.tensor_tensor(kaD[:, mt * TC:(mt + 1) * TC],
                                           PSb(2 + mt % 2, 0, TC), tmpD[:], ALU.mult)
                C.step("vector", v_ka)
            for mt in range(NPT):
                C.step("tensor", lambda t, mt=mt: _last(*[
                    t.matmul(PSb(4 + mt % 2, 0, TC), Wt(wvD, kt, mt),
                             kaD[:, kt * TC:(kt + 1) * TC],
                             start=(kt == 0), stop=(kt == NPT - 1)) for kt in range(NPT)]))
                def v_out(v, mt=mt):
                    return v.tensor_tensor(tmpD[:], PSb(4 + mt % 2, 0, TC),
                                           srD[:, mt * TC:(mt + 1) * TC], ALU.mult)
                C.step("vector", v_out)
                C.step("gpsimd", lambda g, mt=mt, tc=tc: g.dma_start(
                    out=d_xres[:, mt * TP + tc * TC: mt * TP + (tc + 1) * TC],
                    in_=tmpD[:], accum_op=ALU.add), dma=True)
            C.flush()


def emit_output(h):
    nc, p, blk, C, dram = h["nc"], h["p"], h["blk"], h["C"], h["dram"]
    PSb, idf = h["PSb"], h["idf"]
    with contextlib.ExitStack() as s0:
        xT = s0.enter_context(nc.sbuf_tensor("xT_out", [128, NPT * TP], F32))
        yst = s0.enter_context(nc.sbuf_tensor("yst", [128, D], F32))
        C.step("sync", lambda sy: sy.dma_start(out=xT[:], in_=dram["xres"]), dma=True)
        for tt in range(CH // 128):
            def t_tp(t, tt=tt):
                for pt in range(NPT):
                    ins = t.transpose(PSb(pt // 2, (pt % 2) * 256 + 0, 128)
                                      if False else
                                      PSb(pt // 4, (pt % 4) * 128, 128),
                                      xT[:, pt * TP + W + tt * 128: pt * TP + W + (tt + 1) * 128],
                                      idf[:])
                return ins
            C.step("tensor", t_tp)
            def v_cp(v, tt=tt):
                for pt in range(NPT):
                    ins = v.tensor_copy(yst[:, pt * 128:(pt + 1) * 128],
                                        PSb(pt // 4, (pt % 4) * 128, 128))
                return ins
            C.step("vector", v_cp)
            C.step("sync", lambda sy, tt=tt: sy.dma_start(
                out=dram["y"][tt * 128:(tt + 1) * 128, :], in_=yst[:]), dma=True)
        C.flush()


def emit_attention(h, d_kT, d_vh, d_qT, d_oT):
    nc, p, blk, C = h["nc"], h["p"], h["blk"], h["C"]
    PSb, PSf, onesrf = h["PSb"], h["PSf"], h["onesrf"]
    with contextlib.ExitStack() as ph:
        ksb = ph.enter_context(nc.sbuf_tensor("ksb", [128, 2 * S], BF16))
        vsb = ph.enter_context(nc.sbuf_tensor("vsb", [128, 2 * NKT * 130], BF16))
        qhp = ph.enter_context(nc.sbuf_tensor("qhp", [128, 2 * TP], BF16))
        ohp = ph.enter_context(nc.sbuf_tensor("ohp", [128, 2 * TP], BF16))
        pb = ph.enter_context(nc.sbuf_tensor("pb", [128, 3 * 2 * TC], BF16))
        recr = ph.enter_context(nc.sbuf_tensor("recr", [1, 2 * TC], F32))
        brs = ph.enter_context(nc.sbuf_tensor("brs", [128, 2 * TC], F32))
        sxp, spv, sqn, srec, sbrr = (p.sem(x) for x in
                                     ("sxp", "spv", "sqn", "srec", "sbrr"))
        sost = [p.sem("sost0"), p.sem("sost1")]
        sqk = p.sem("sqk")
        skl = [p.sem("skl0"), p.sem("skl1")]
        C.flush()
        tl = C.tail()
        KSB = lambda hp: ksb[:, (hp % 2) * S:(hp % 2 + 1) * S]
        VSB = lambda hp: vsb[:, (hp % 2) * NKT * 130:(hp % 2 + 1) * NKT * 130]
        QHP = lambda hp: qhp[:, (hp % 2) * TP:(hp % 2 + 1) * TP]
        OHP = lambda hp: ohp[:, (hp % 2) * TP:(hp % 2 + 1) * TP]

        @blk.sync
        def _(sy):
            if tl:
                sy.wait_ge(p.sem(tl[0]), tl[1])
            for hp in range(NHP):
                if hp >= 2:
                    sy.wait_ge(spv, (hp - 1) * NTC * NKT)
                sy.dma_start(out=KSB(hp), in_=d_kT[hp * 128:(hp + 1) * 128, :]
                             ).then_inc(skl[hp % 2], 16)
                sy.dma_start(out=VSB(hp).rearrange("p (t c) -> p t c", c=130),
                             in_=d_vh[hp].rearrange("(t p) c -> p t c", p=128)
                             ).then_inc(skl[hp % 2], 16)
                sy.dma_start(out=QHP(hp), in_=d_qT[hp * 128:(hp + 1) * 128, :]
                             ).then_inc(skl[hp % 2], 16)
                if hp >= 1:
                    sy.wait_ge(sqn, hp * NTC)
                    sy.dma_start(out=d_oT[(hp - 1) * 128:hp * 128, :], in_=OHP(hp - 1)
                                 ).then_inc(sost[(hp - 1) % 2], 16)
            sy.wait_ge(sqn, NHP * NTC)
            sy.dma_start(out=d_oT[(NHP - 1) * 128:NHP * 128, :], in_=OHP(NHP - 1)
                         ).then_inc(sost[(NHP - 1) % 2], 16)

        @blk.tensor
        def _(t):
            if tl:
                t.wait_ge(p.sem(tl[0]), tl[1])
            nqk = 0
            for hp in range(NHP):
                t.wait_ge(skl[hp % 2], (hp // 2 + 1) * 48)
                for qc in range(NTC):
                    it = hp * NTC + qc

                    def emit_pv(j, it=it, hp=hp):
                        slot = j % 3
                        t.wait_ge(sxp, it * NKT + j + 1)
                        if j == 0 and it > 0:
                            t.wait_ge(sqn, it)
                        t.matmul(PSb(6, 0, TC)[0:65, :],
                                 VSB(hp)[:, j * 130: j * 130 + 65],
                                 pb[:, (slot * 2) * TC:(slot * 2 + 1) * TC],
                                 start=(j == 0), stop=(j == NKT - 1),
                                 skip_group_check=True)
                        t.matmul(PSb(6, 0, TC)[96:97, :],
                                 VSB(hp)[:, j * 130 + 64: j * 130 + 65],
                                 pb[:, (slot * 2 + 1) * TC:(slot * 2 + 2) * TC],
                                 start=(j == 0), stop=(j == NKT - 1),
                                 tile_position=(0, 96), skip_group_check=True)
                        t.matmul(PSb(7, 0, TC)[64:128, :],
                                 VSB(hp)[:, j * 130 + 65: j * 130 + 129],
                                 pb[:, (slot * 2 + 1) * TC:(slot * 2 + 2) * TC],
                                 start=(j == 0), stop=(j == NKT - 1),
                                 tile_position=(0, 64),
                                 skip_group_check=True).then_inc(spv, 1)

                    for kt in range(NKT):
                        slot = kt % 3
                        if nqk >= 3:
                            t.wait_ge(sxp, nqk - 2)
                        if kt == 0 and it > 0:
                            t.wait_ge(sqn, it)
                        t.matmul(PSb(2 * slot, 0, TC),
                                 KSB(hp)[0:64, kt * 128:(kt + 1) * 128],
                                 QHP(hp)[0:64, qc * TC:(qc + 1) * TC],
                                 start=True, stop=True, skip_group_check=True)
                        t.matmul(PSb(2 * slot + 1, 0, TC),
                                 KSB(hp)[64:128, kt * 128:(kt + 1) * 128],
                                 QHP(hp)[64:128, qc * TC:(qc + 1) * TC],
                                 start=True, stop=True, tile_position=(64, 0),
                                 skip_group_check=True).then_inc(sqk, 1)
                        nqk += 1
                        if kt >= 1:
                            emit_pv(kt - 1)
                    emit_pv(NKT - 1)
                    t.wait_ge(srec, it + 1)
                    t.matmul(PSb(0, 0, TC), onesrf[:], recr[0:1, 0:TC],
                             start=True, stop=True, skip_group_check=True)
                    t.matmul(PSb(1, 0, TC), onesrf[:], recr[0:1, TC:2 * TC],
                             start=True, stop=True,
                             skip_group_check=True).then_inc(sbrr, 1)

        @blk.scalar
        def _(s):
            nxp = 0
            for hp in range(NHP):
                for qc in range(NTC):
                    for kt in range(NKT):
                        slot = kt % 3
                        s.wait_ge(sqk, nxp + 1)
                        if nxp >= 3:
                            s.wait_ge(spv, nxp - 2)
                        s.activation(
                            pb[:, slot * 2 * TC:(slot + 1) * 2 * TC].rearrange(
                                "p (b n) -> p b n", b=2),
                            PSf[:, 2 * slot * BANK:(2 * slot + 2) * BANK].rearrange(
                                "p (b n) -> p b n", b=2)[:, :, 0:TC],
                            AF.Exp, scale=0.125).then_inc(sxp, 1)
                        nxp += 1

        @blk.vector
        def _(v):
            for hp in range(NHP):
                for qc in range(NTC):
                    it = hp * NTC + qc
                    if qc == 0 and hp >= 2:
                        v.wait_ge(sost[hp % 2], ((hp - 2) // 2 + 1) * 16)
                    v.wait_ge(spv, (it + 1) * NKT)
                    v.reciprocal(recr[0:1, 0:TC], PSb(6, 0, TC)[64:65, :])
                    v.reciprocal(recr[0:1, TC:2 * TC],
                                 PSb(6, 0, TC)[96:97, :]).then_inc(srec, 1)
                    v.wait_ge(sbrr, it + 1)
                    ia = v.tensor_copy(brs[0:64, 0:TC], PSb(0, 0, TC)[0:64, :])
                    ib = v.tensor_copy(brs[64:128, TC:2 * TC], PSb(1, 0, TC)[64:128, :])
                    p.vg(v, ia)
                    p.vg(v, ib)
                    v.tensor_tensor(OHP(hp)[0:64, qc * TC:(qc + 1) * TC],
                                    PSb(6, 0, TC)[0:64, :], brs[0:64, 0:TC], ALU.mult)
                    v.tensor_tensor(OHP(hp)[64:128, qc * TC:(qc + 1) * TC],
                                    PSb(7, 0, TC)[64:128, :], brs[64:128, TC:2 * TC],
                                    ALU.mult).then_inc(sqn, 1)

        C.step("vector", lambda v: _last(v.wait_ge(sost[0], (NHP // 2) * 16),
                                         v.wait_ge(sost[1], (NHP // 2) * 16),
                                         v.tensor_copy(brs[0:1, 0:1], brs[0:1, 0:1])))
        C.flush()



def _prep_core_inputs(c, full):
    b, j = c // 4, c % 4
    x = full["L0_attn_x"][b]
    lo = j * CH - W
    if lo < 0:
        xs = np.concatenate([np.zeros((W, D), np.float32), np.asarray(x[0:CH])], 0)
        mask = np.concatenate([np.zeros((128, W), np.float32),
                               np.ones((128, CH), np.float32)], 1)
    else:
        xs = np.ascontiguousarray(x[lo: (j + 1) * CH])
        mask = np.ones((128, TP), np.float32)
    T = lambda a: np.ascontiguousarray(np.asarray(a).T)
    A = lambda a: np.ascontiguousarray(np.asarray(a))
    return {
        "xs": np.ascontiguousarray(xs),
        "xb": A(x),
        "mask": mask,
        "ident": np.eye(128, dtype=np.float32),
        "w_q": A(full["L0_attn_w_q"]),
        "w_k": A(full["L0_attn_w_k"]),
        "w_v": A(full["L0_attn_w_v"]),
        "w_proj": A(full["L0_attn_w_proj"]),
        "b_proj": A(full["L0_attn_b_proj"]),
        "ln_g": A(full["L0_attn_ln_gamma"]),
        "ln_b": A(full["L0_attn_ln_beta"]),
        "sh_gate_T": T(full["L0_ffn_shared_gate"]),
        "sh_up_T": T(full["L0_ffn_shared_up"]),
        "sh_down_T": T(full["L0_ffn_shared_down"]),
        "gate_w": A(full["L0_ffn_gate_w"]),
        "up_w": A(full["L0_ffn_up_w"]),
        "down_w": A(full["L0_ffn_down_w"]),
        "router_T": T(full["L0_ffn_router_w"]),
        "r_bias": A(full["L0_ffn_routing_bias"]),
        "l1_w": A(full["L1_attn_w"]),
        "l1_u": A(full["L1_attn_u"]),
        "l1_mk": A(full["L1_attn_mix_k"]),
        "l1_mv": A(full["L1_attn_mix_v"]),
        "l1_mr": A(full["L1_attn_mix_r"]),
        "Wk_T": T(full["L1_attn_Wk"]),
        "Wv_T": T(full["L1_attn_Wv"]),
        "Wr_T": T(full["L1_attn_Wr"]),
        "Wo_T": T(full["L1_attn_Wo"]),
        "f_Wr_T": T(full["L1_ffn_W_r"]),
        "f_Wk_T": T(full["L1_ffn_W_k"]),
        "f_Wv_T": T(full["L1_ffn_W_v"]),
        "f_mur": A(full["L1_ffn_mu_r"]),
        "f_muk": A(full["L1_ffn_mu_k"]),
    }


ALL_STAGES = ("IN", "A", "B", "C", "D", "OUT")


def kernel(**inputs):
    nc, out_names = build_program(stages=ALL_STAGES)
    in_maps = [_prep_core_inputs(c, inputs) for c in range(8)]
    res = run_bass_kernel_spmd(nc, in_maps, core_ids=list(range(8)))
    out = np.zeros((B, S, D), np.float32)
    for c in range(8):
        b, j = c // 4, c % 4
        out[b, j * CH:(j + 1) * CH, :] = res.results[c]["y"]
    return out
